# version 04
# baseline (speedup 1.0000x reference)
import math
import contextlib

import numpy as np
import ml_dtypes

import concourse.bass as bass
import concourse.mybir as mybir
from concourse.bass_utils import run_bass_kernel_spmd

F32 = mybir.dt.float32
BF16 = mybir.dt.bfloat16
AF = mybir.ActivationFunctionType
ALU = mybir.AluOpType
AX = mybir.AxisListType

L, D, NH, HE = 4096, 1024, 16, 64
C = 128
SC = 512
NSC = L // SC
NCC = SC // C
IFF = 1408
NIT = IFF // 128
NEXP = 9
NTILE = L // SC
EPS = 1e-5
GNN = float(HE * L)
ENGINES = ("tensor", "vector", "scalar", "sync", "gpsimd")
PPT = 3 * NEXP
NPIECES = PPT * NTILE
WSLOT = 8 * IFF


def _gammas64():
    return np.array([1.0 - 2.0 ** (-5 - math.floor(8 * i / NH)) for i in range(NH)],
                    dtype=np.float64)


class Sem:
    def __init__(self, nc, es, name):
        self.h = es.enter_context(nc.semaphore(name))
        self.n = 0


class Emitter:
    def __init__(self, nc):
        self.nc = nc
        self.q = {k: [] for k in ENGINES}

    def __call__(self, eng, fn):
        self.q[eng].append(fn)

    def flush_block(self):
        with self.nc.Block() as block:
            for name in ENGINES:
                fns = self.q[name]
                if fns:
                    getattr(block, name)(lambda e, fns=tuple(fns): [f(e) for f in fns])
        self.q = {k: [] for k in ENGINES}


def build_nc(dbg=False, phases=(1, 2), nsc=None, p1_att=True, p1_qkv=True, p1_gn=True, p1_stats=True,
             use_conv=True):
    if nsc is None:
        nsc = NSC
    nc = bass.Bass("TRN2", target_bir_lowering=False)

    x_h = nc.dram_tensor("x", [L, D], F32, kind="ExternalInput").ap()
    wq_h = nc.dram_tensor("wq", [D, D], F32, kind="ExternalInput")
    wk_h = nc.dram_tensor("wk", [D, D], F32, kind="ExternalInput")
    wv_h = nc.dram_tensor("wv", [D, D], F32, kind="ExternalInput")
    wout_h = nc.dram_tensor("wout", [D, D], F32, kind="ExternalInput")
    gate9_h = nc.dram_tensor("gate9", [D, 9], F32, kind="ExternalInput")
    wg_h = nc.dram_tensor("wg", [8, D, IFF], F32, kind="ExternalInput")
    wu_h = nc.dram_tensor("wu", [8, D, IFF], F32, kind="ExternalInput")
    wd_h = nc.dram_tensor("wd", [8, IFF, D], F32, kind="ExternalInput")
    swg_h = nc.dram_tensor("swg", [D, IFF], F32, kind="ExternalInput")
    swu_h = nc.dram_tensor("swu", [D, IFF], F32, kind="ExternalInput")
    swd_h = nc.dram_tensor("swd", [IFF, D], F32, kind="ExternalInput")
    dmask_h = nc.dram_tensor("dmask", [C, NH * C], F32, kind="ExternalInput").ap()
    gcols_h = nc.dram_tensor("gcols", [C, 2 * NH], F32, kind="ExternalInput").ap()
    gnwb_h = nc.dram_tensor("gnwb", [C, 16], F32, kind="ExternalInput").ap()
    eye_h = nc.dram_tensor("eye", [C, C], BF16, kind="ExternalInput").ap()
    out_h = nc.dram_tensor("out", [L, D], F32, kind="ExternalOutput").ap()
    comb_h = nc.dram_tensor("comb", [L, D], BF16,
                            kind=("ExternalOutput" if dbg else "Internal")).ap()
    wbf_h = nc.dram_tensor("wbf", [PPT, C, WSLOT], BF16, kind="Internal").ap()

    g64 = _gammas64()
    gC_imm = [float(np.float32(g64[h] ** C)) for h in range(NH)]

    es = contextlib.ExitStack()
    sb = lambda name, shape, dt: es.enter_context(nc.sbuf_tensor(name, shape, dt))

    eye = sb("eye_sb", [C, C], BF16)
    gnwb = sb("gnwb_sb", [C, 16], F32)
    wout_bf = sb("wout_bf", [C, 8 * D], BF16)
    gate9_bf = sb("gate9_bf", [C, 8 * 9], BF16)
    x_sb = sb("x_sb", [C, 4 * D], F32)
    cols = sb("cols", [C, 32], F32)
    sqdump = sb("sqdump", [C, 2 * D], BF16)
    statm = sb("statm", [C, 32], F32)
    srow = sb("srow", [1, 32], F32)
    srow2 = sb("srow2", [1, 16], F32)
    bcast = sb("bcast", [C, 32], F32)
    s12 = sb("s12", [C, 16], F32)
    onecol = sb("onecol", [C, 1], F32)
    onerow = sb("onerow", [1, C], F32)
    cst = sb("cst", [C, 3], F32)

    sems = {}
    for name in ("sld", "gld", "sst", "vc1", "sc1", "pet", "vct", "peq", "vcq",
                 "pea", "vca", "vst", "scst", "pgn", "vgn", "scg2",
                 "pew", "vcw", "per", "vcr", "scr", "pe2", "scg", "vch",
                 "ped", "vcd", "plds0", "plds1", "plds2",
                 "gcv0", "gcv1", "gcs0", "gcs1", "sstA", "sstB",
                 "scku", "sctb", "scq"):
        sems[name] = Sem(nc, es, name)
    S = lambda k: sems[k]

    em = Emitter(nc)

    def emit_ln_sc(src, dst, per_tt_wait=None):
        ss = S("sc1")

        def s_ln(sc):
            for tt in range(4):
                if per_tt_wait is not None:
                    w = per_tt_wait[tt]
                    sc.wait_ge(S(w[0]).h, w[1])
                if tt >= 2:
                    sc.drain()
                sc.activation(sqdump[:, (tt % 2) * D:(tt % 2 + 1) * D],
                              src[:, tt * D:(tt + 1) * D], AF.Identity,
                              accum_out=cols[:, tt:tt + 1])
                sc.drain()
                sc.activation(sqdump[:, (tt % 2) * D:(tt % 2 + 1) * D],
                              src[:, tt * D:(tt + 1) * D], AF.Square,
                              accum_out=cols[:, 4 + tt:5 + tt])
            sc.drain()
            sc.mul(cols[:, 8:12], cols[:, 0:4], 1.0 / D)
            sc.drain()
            for tt in range(4):
                sc.mul(cols[:, 16 + tt:17 + tt], cols[:, 8 + tt:9 + tt],
                       cols[:, 8 + tt:9 + tt])
            sc.drain()
            sc.mul(cols[:, 16:20], cols[:, 16:20], -1.0)
            sc.mul(cols[:, 20:24], cols[:, 4:8], 1.0 / D)
            sc.drain()
            for tt in range(4):
                sc.add(cols[:, 20 + tt:21 + tt], cols[:, 20 + tt:21 + tt],
                       cols[:, 16 + tt:17 + tt])
            sc.drain()
            sc.activation(cols[:, 16:20], cols[:, 20:24], AF.Ln, bias=cst[:, 0:1])
            sc.drain()
            sc.activation(cols[:, 12:16], cols[:, 16:20], AF.Exp, scale=cst[:, 1:2])
            sc.mul(cols[:, 24:28], cols[:, 8:12], -1.0)
            sc.drain()
            for tt in range(4):
                sc.mul(cols[:, 24 + tt:25 + tt], cols[:, 24 + tt:25 + tt],
                       cols[:, 12 + tt:13 + tt])
            sc.drain()
            for tt in range(4):
                ins = sc.activation(dst[:, tt * D:(tt + 1) * D],
                                    src[:, tt * D:(tt + 1) * D], AF.Identity,
                                    scale=cols[:, 12 + tt:13 + tt],
                                    bias=cols[:, 24 + tt:25 + tt])
            ins.then_inc(ss.h, 1)
        ss.n += 1
        em("scalar", s_ln)
        return ss.n

    with contextlib.ExitStack() as p1es:
        sb1 = lambda name, shape, dt: p1es.enter_context(nc.sbuf_tensor(name, shape, dt))
        ps1 = lambda name, shape, dt: p1es.enter_context(nc.psum_tensor(name, shape, dt))

        dmask = sb1("dmask_sb", [C, NH * C], F32)
        gcols = sb1("gcols_sb", [C, 2 * NH], F32)
        wq_bf = sb1("wq_bf", [C, 8 * D], BF16)
        wk_bf = sb1("wk_bf", [C, 8 * D], BF16)
        wv_bf = sb1("wv_bf", [C, 8 * D], BF16)
        xln_bf = sb1("xln_bf", [C, 4 * D], BF16)
        xlnT = sb1("xlnT", [C, 8 * SC], BF16)
        qt_sb = sb1("qt_sb", [C, 8 * SC], BF16)
        kt_sb = sb1("kt_sb", [C, 8 * SC], BF16)
        qt2_sb = sb1("qt2_sb", [HE, 8 * SC], BF16)
        kt2_sb = sb1("kt2_sb", [HE, 8 * SC], BF16)
        v_sb = sb1("v_sb", [C, 4 * D], BF16)
        comb_sb = sb1("comb_sb", [C, 4 * D], BF16)
        tf32 = sb1("tf32", [HE, NH * HE], F32)
        tbf = sb1("tbf", [HE, NH * HE], BF16)
        am_sb = sb1("am_sb", [C, 4 * C], BF16)
        ku_sb = sb1("ku_sb", [C, 2 * 4 * HE], BF16)
        oicp = sb1("oicp", [C, 2 * 4 * HE], F32)
        stattmp = sb1("stattmp", [C, 32], F32)
        stg = sb1("stg", [C, WSLOT], BF16)

        psf = ps1("psf", [C, 6 * 512], F32)
        psb = ps1("psb", [C, 2048], BF16)

        bank = lambda i: psf[:, i * 512:(i + 1) * 512]

        def tsl(h):
            return slice(0, HE), slice(h * HE, (h + 1) * HE)

        def hsl(t_main, t_alt, h, ccx):
            src_t = t_main if h % 2 == 0 else t_alt
            return src_t[0:HE, (h // 2) * SC + ccx * C:(h // 2) * SC + (ccx + 1) * C]

        S("sld").n += 64

        def p0_sync(s, tgt=S("sld").n):
            for dst, src in ((dmask, dmask_h), (gcols, gcols_h), (gnwb, gnwb_h), (eye, eye_h)):
                ins = s.dma_start(out=dst[:, :], in_=src)
                ins.then_inc(S("sld").h, 16)
            s.wait_ge(S("sld").h, tgt)
        em("sync", p0_sync)

        S("gld").n += 80

        def p0_gp(g, tgt=S("gld").n):
            for dst, src in ((wq_bf, wq_h), (wk_bf, wk_h), (wv_bf, wv_h), (wout_bf, wout_h)):
                g.dma_start(out=dst[:, :],
                            in_=bass.AP(src, 0, [[D, C], [C * D, 8], [1, D]])
                            ).then_inc(S("gld").h, 16)
            g.dma_start(out=gate9_bf[:, :],
                        in_=bass.AP(gate9_h, 0, [[9, C], [C * 9, 8], [1, 9]])
                        ).then_inc(S("gld").h, 16)
            g.wait_ge(S("gld").h, tgt)
        em("gpsimd", p0_gp)

        def p0_vec(v):
            v.memset(tf32[:, :], 0.0)
            v.memset(tbf[:, :], 0.0)
            v.memset(statm[:, :], 0.0)
            v.memset(onecol[:, :], 1.0)
            v.memset(onerow[:, :], 1.0)
            v.memset(cst[:, 0:1], EPS)
            v.memset(cst[:, 1:2], -0.5)
            v.memset(cst[:, 2:3], -1.0)
        em("vector", p0_vec)
        em.flush_block()

        def emit_t512a(src, dst, wait_sem=None, wait_tgt=0):
            pe, vc = S("pet"), S("vct")
            for dt in range(8):
                reg = psb[:, (dt % 2) * 1024:(dt % 2) * 1024 + 512]

                def ttr(t, dt=dt, reg=reg, w=(wait_sem if dt == 0 else None),
                        wt=wait_tgt, banktgt=vc.n - 1):
                    if w is not None:
                        t.wait_ge(S(w).h, wt)
                    if banktgt >= 1:
                        t.wait_ge(vc.h, banktgt)
                    for tt in range(4):
                        ins = t.transpose(reg[:, tt * C:(tt + 1) * C],
                                          src[:, tt * D + dt * C:tt * D + (dt + 1) * C],
                                          eye[:, :])
                    ins.then_inc(pe.h, 1)
                pe.n += 1
                em("tensor", ttr)

                def tev(v, dt=dt, reg=reg, tgt=pe.n):
                    v.wait_ge(pe.h, tgt)
                    ins = v.tensor_copy(dst[:, dt * SC:(dt + 1) * SC], reg[:, :])
                    ins.then_inc(vc.h, 1)
                vc.n += 1
                em("vector", tev)

        def conv_src(p):
            e, j = p // 3, p % 3
            if e < 8:
                hs = (wg_h, wu_h, wd_h)[j]
                off = e * D * IFF
            else:
                hs = (swg_h, swu_h, swd_h)[j]
                off = 0
            if j < 2:
                return bass.AP(hs, off, [[IFF, C], [C * IFF, 8], [1, IFF]])
            return bass.AP(hs, off, [[D, C], [C * D, NIT], [1, D]])

        conv_state = {"p": 0}

        def emit_conversions(n):
            def gconv(g, plist=tuple(range(conv_state["p"], min(conv_state["p"] + n, PPT)))):
                for p in plist:
                    if p >= 1:
                        g.wait_ge(S("gcs0").h, 16 * p)
                    g.dma_start(out=stg[:, :], in_=conv_src(p)).then_inc(S("gcv0").h, 16)
                    g.wait_ge(S("gcv0").h, 16 * (p + 1))
                    g.dma_start(out=wbf_h[p], in_=stg[:, :]).then_inc(S("gcs0").h, 16)
            for p in range(conv_state["p"], min(conv_state["p"] + n, PPT)):
                S("gcv0").n += 16
                S("gcs0").n += 16
            conv_state["p"] = min(conv_state["p"] + n, PPT)
            em("gpsimd", gconv)

        am2 = sb1("am2", [C, 2 * 4 * C], BF16)
        S("sld").n += 64

        def ldx0(s, tgt=S("sld").n):
            for tt in range(4):
                ins = s.dma_start(out=x_sb[:, tt * D:(tt + 1) * D],
                                  in_=x_h[tt * C:(tt + 1) * C, :])
                ins.then_inc(S("sld").h, 16)
            s.wait_ge(S("sld").h, tgt)
        em("sync", ldx0)

        att_done_prev = 0
        spill_tgt_prev = 0
        att_state = {}
        for c in range(nsc):
            if use_conv:
                emit_conversions(4)
            xtgt = S("sld").n

            em("scalar", lambda sc, tgt=xtgt: (sc.wait_ge(S("sld").h, tgt), sc.drain()))

            ln_done = emit_ln_sc(x_sb, xln_bf)

            if c + 1 < nsc:
                S("sld").n += 64

                def ldx(s, c=c, vtgt=ln_done, tgt=S("sld").n):
                    s.wait_ge(S("sc1").h, vtgt)
                    for tt in range(4):
                        ins = s.dma_start(out=x_sb[:, tt * D:(tt + 1) * D],
                                          in_=x_h[(c + 1) * SC + tt * C:(c + 1) * SC + (tt + 1) * C, :])
                        ins.then_inc(S("sld").h, 16)
                    s.wait_ge(S("sld").h, tgt)
                em("sync", ldx)

            em("tensor", lambda t, tgt=att_done_prev: (t.wait_ge(S("vca").h, tgt)
                                                       if tgt else None))
            emit_t512a(xln_bf, xlnT, wait_sem="sc1", wait_tgt=ln_done)
            tr_done = S("vct").n

            groups = ([("q", i) for i in range(8)] + [("k", i) for i in range(8)]
                      + [("v", i) for i in range(8)])
            evac_tgts = []
            q_evac = {}
            k_evac = {}
            for gidx, (kind, idx) in enumerate(groups):
                bk = bank(gidx % 2)

                def tqkv(t, kind=kind, idx=idx, bk=bk, gidx=gidx,
                         trtgt=tr_done,
                         banktgt=(evac_tgts[gidx - 2] if gidx >= 2 else None),
                         atgt=att_done_prev):
                    if gidx == 0:
                        t.wait_ge(S("vct").h, trtgt)
                        if atgt:
                            t.wait_ge(S("vca").h, atgt)
                    if banktgt is not None:
                        t.wait_ge(S(banktgt[0]).h, banktgt[1])
                    for dt in range(8):
                        if kind == "q":
                            w = wq_bf[:, dt * D + idx * C:dt * D + (idx + 1) * C]
                            mv = xlnT[:, dt * SC:(dt + 1) * SC]
                        elif kind == "k":
                            w = wk_bf[:, dt * D + idx * C:dt * D + (idx + 1) * C]
                            mv = xlnT[:, dt * SC:(dt + 1) * SC]
                        else:
                            tt, ec = idx // 2, idx % 2
                            w = xlnT[:, dt * SC + tt * C:dt * SC + (tt + 1) * C]
                            mv = wv_bf[:, dt * D + ec * 512:dt * D + (ec + 1) * 512]
                        ins = t.matmul(bk, w, mv, start=(dt == 0), stop=(dt == 7))
                    ins.then_inc(S("peq").h, 1)
                S("peq").n += 1
                em("tensor", tqkv)

                if kind == "q":
                    def vqkv(v, idx=idx, bk=bk, tgt=S("peq").n):
                        v.wait_ge(S("peq").h, tgt)
                        ins = v.tensor_copy(qt_sb[:, idx * SC:(idx + 1) * SC], bk)
                        ins.then_inc(S("vcq").h, 1)
                    S("vcq").n += 1
                    evac_tgts.append(("vcq", S("vcq").n))
                    q_evac[idx] = ("vcq", S("vcq").n)
                    em("vector", vqkv)
                else:
                    def sqkv(sc, kind=kind, idx=idx, bk=bk, tgt=S("peq").n):
                        sc.wait_ge(S("peq").h, tgt)
                        if kind == "k":
                            dst = kt_sb[:, idx * SC:(idx + 1) * SC]
                        else:
                            tt, ec = idx // 2, idx % 2
                            dst = v_sb[:, tt * D + ec * 512:tt * D + (ec + 1) * 512]
                        ins = sc.copy(dst, bk)
                        ins.then_inc(S("scq").h, 1)
                    S("scq").n += 1
                    evac_tgts.append(("scq", S("scq").n))
                    if kind == "k":
                        k_evac[idx] = ("scq", S("scq").n)
                    em("scalar", sqkv)
            qkv_done_v = S("vcq").n
            qkv_done_s = S("scq").n

            for et in range(8):
                S("sld").n += 32

                def qk2(s, et=et, qw=q_evac[et], kw=k_evac[et], tgt=S("sld").n):
                    s.wait_ge(S(qw[0]).h, qw[1])
                    s.wait_ge(S(kw[0]).h, kw[1])
                    s.dma_start(out=qt2_sb[:, et * SC:(et + 1) * SC],
                                in_=qt_sb[HE:2 * HE, et * SC:(et + 1) * SC]).then_inc(S("sld").h, 16)
                    ins = s.dma_start(out=kt2_sb[:, et * SC:(et + 1) * SC],
                                      in_=kt_sb[HE:2 * HE, et * SC:(et + 1) * SC])
                    ins.then_inc(S("sld").h, 16)
                    s.wait_ge(S("sld").h, tgt)
                em("sync", qk2)
            qk2_done = S("sld").n

            abase = S("vca").n
            pea_p1 = {}
            pea_p2 = {}
            vca_v1 = {}
            vca_v2 = {}
            scku_m = {}
            sctb_m = att_state.setdefault("sctb_m", {})
            ab_iter = att_state.setdefault("ab_iter", [0])

            def emit_p1(m):
                ccx, hg = m // 4, m % 4
                heads = [hg * 4 + i for i in range(4)]
                p = m % 2
                a_bank = bank(2 if p == 0 else 5)
                ktr0 = 512 + p * 1024

                def t_p1(t, ccx=ccx, heads=heads, a_bank=a_bank, ktr0=ktr0, m=m,
                         qtgt=qkv_done_v, stgt=qkv_done_s, q2tgt=qk2_done,
                         atgt=att_done_prev, vtgt=(vca_v1.get(m - 2, 0)),
                         kutgt=scku_m.get(m - 2, 0)):
                    if m < 2:
                        t.wait_ge(S("vcq").h, qtgt)
                        t.wait_ge(S("scq").h, stgt)
                        t.wait_ge(S("sld").h, q2tgt)
                        if atgt:
                            t.wait_ge(S("vca").h, atgt)
                    else:
                        t.wait_ge(S("vca").h, vtgt)
                        t.wait_ge(S("scku").h, kutgt)
                    for hi, h in enumerate(heads):
                        kt = hsl(kt_sb, kt2_sb, h, ccx)
                        qt = hsl(qt_sb, qt2_sb, h, ccx)
                        t.matmul(a_bank[:, hi * C:(hi + 1) * C], kt, qt,
                                 start=True, stop=True)
                    for hi, h in enumerate(heads):
                        kt = hsl(kt_sb, kt2_sb, h, ccx)
                        ins = t.transpose(psb[:, ktr0 + hi * HE:ktr0 + (hi + 1) * HE],
                                          kt, eye[0:HE, 0:HE])
                    ins.then_inc(S("pea").h, 1)
                S("pea").n += 1
                pea_p1[m] = S("pea").n
                em("tensor", t_p1)

            def emit_p2(m):
                ccx, hg = m // 4, m % 4
                heads = [hg * 4 + i for i in range(4)]
                p = m % 2
                oio_bank = bank(3 if p == 0 else 0)
                u_bank = bank(4 if p == 0 else 1)
                amr = am2[:, p * 4 * C:(p + 1) * 4 * C]

                kur0 = (m % 2) * 4 * HE

                def t_p2(t, ccx=ccx, heads=heads, oio=oio_bank, ub=u_bank, amr=amr,
                         kur0=kur0, vtgt=vca_v1[m], kutgt=scku_m[m],
                         tbtgt=sctb_m.get(ab_iter[0] + m - 4, 0)):
                    t.wait_ge(S("vca").h, vtgt)
                    t.wait_ge(S("scku").h, kutgt)
                    if tbtgt:
                        t.wait_ge(S("sctb").h, tbtgt)
                    for hi, h in enumerate(heads):
                        t.matmul(oio[:, hi * HE:(hi + 1) * HE],
                                 amr[:, hi * C:(hi + 1) * C],
                                 v_sb[:, ccx * D + h * HE:ccx * D + (h + 1) * HE],
                                 start=True, stop=True)
                    for hi, h in enumerate(heads):
                        tr, tc = tsl(h)
                        qt = hsl(qt_sb, qt2_sb, h, ccx)
                        t.matmul(oio[:, 256 + hi * HE:256 + (hi + 1) * HE], qt,
                                 tbf[tr, tc], start=True, stop=True)
                    for hi, h in enumerate(heads):
                        ins = t.matmul(ub[0:HE, hi * HE:(hi + 1) * HE],
                                       ku_sb[:, kur0 + hi * HE:kur0 + (hi + 1) * HE],
                                       v_sb[:, ccx * D + h * HE:ccx * D + (h + 1) * HE],
                                       start=True, stop=True)
                    ins.then_inc(S("pea").h, 1)
                S("pea").n += 1
                pea_p2[m] = S("pea").n
                em("tensor", t_p2)

            def emit_v1(m):
                ccx, hg = m // 4, m % 4
                heads = [hg * 4 + i for i in range(4)]
                p = m % 2
                a_bank = bank(2 if p == 0 else 5)
                ktr0 = 512 + p * 1024
                amr = am2[:, p * 4 * C:(p + 1) * 4 * C]
                kur0 = p * 4 * HE

                def v_1(v, hg=hg, heads=heads, a_bank=a_bank, amr=amr,
                        tgt=pea_p1[m], stgt=(spill_tgt_prev if m == 0 else 0)):
                    v.wait_ge(S("pea").h, tgt)
                    if stgt:
                        v.wait_ge(S("sst").h, stgt)
                    ins = v.tensor_tensor(amr[:, :], a_bank,
                                          dmask[:, hg * 4 * C:(hg * 4 + 4) * C], ALU.mult)
                    ins.then_inc(S("vca").h, 1)
                S("vca").n += 1
                vca_v1[m] = S("vca").n
                em("vector", v_1)

                def s_ku(sc, heads=heads, ktr0=ktr0, kur0=kur0, tgt=pea_p1[m]):
                    sc.wait_ge(S("pea").h, tgt)
                    sc.drain()
                    for hi, h in enumerate(heads):
                        ins = sc.mul(ku_sb[:, kur0 + hi * HE:kur0 + (hi + 1) * HE],
                                     psb[:, ktr0 + hi * HE:ktr0 + (hi + 1) * HE],
                                     gcols[:, NH + h:NH + h + 1])
                    ins.then_inc(S("scku").h, 1)
                S("scku").n += 1
                scku_m[m] = S("scku").n
                em("scalar", s_ku)

            def emit_v2(m):
                ccx, hg = m // 4, m % 4
                heads = [hg * 4 + i for i in range(4)]
                p = m % 2
                oio_bank = bank(3 if p == 0 else 0)
                u_bank = bank(4 if p == 0 else 1)

                def v_2(v, ccx=ccx, heads=heads, oio=oio_bank, ub=u_bank,
                        oir0=(m % 2) * 4 * HE, tgt=pea_p2[m]):
                    v.wait_ge(S("pea").h, tgt)
                    v.tensor_copy(oicp[:, oir0:oir0 + 4 * HE], oio[:, 0:4 * HE])
                    v.drain()
                    for hi, h in enumerate(heads):
                        v.scalar_tensor_tensor(
                            comb_sb[:, ccx * D + h * HE:ccx * D + (h + 1) * HE],
                            oio[:, 256 + hi * HE:256 + (hi + 1) * HE],
                            gcols[:, h:h + 1],
                            oicp[:, oir0 + hi * HE:oir0 + (hi + 1) * HE],
                            ALU.mult, ALU.add)
                    for hi, h in enumerate(heads):
                        tr, tc = tsl(h)
                        ins = v.scalar_tensor_tensor(tf32[tr, tc], tf32[tr, tc], gC_imm[h],
                                                     ub[0:HE, hi * HE:(hi + 1) * HE],
                                                     ALU.mult, ALU.add)
                    ins.then_inc(S("vca").h, 1)
                S("vca").n += 1
                vca_v2[m] = S("vca").n
                em("vector", v_2)

                def s_tb(sc, heads=heads, tgt=vca_v2[m]):
                    sc.wait_ge(S("vca").h, tgt)
                    sc.drain()
                    for hi, h in enumerate(heads):
                        tr, tc = tsl(h)
                        ins = sc.copy(tbf[tr, tc], tf32[tr, tc])
                    ins.then_inc(S("sctb").h, 1)
                S("sctb").n += 1
                sctb_m[ab_iter[0] + m] = S("sctb").n
                em("scalar", s_tb)

            for m in range(16):
                emit_p1(m)
                emit_v1(m)
                if m >= 1:
                    emit_p2(m - 1)
                    emit_v2(m - 1)
            emit_p2(15)
            emit_v2(15)
            ab_iter[0] += 16
            att_done = S("vca").n
            att_done_prev = att_done

            S("sst").n += 64
            spill_tgt_prev = S("sst").n

            def spill(s, c=c, tgt=att_done, stgt=S("sst").n):
                s.wait_ge(S("vca").h, tgt)
                for tt in range(4):
                    ins = s.dma_start(out=comb_h[c * SC + tt * C:c * SC + (tt + 1) * C, :],
                                      in_=comb_sb[:, tt * D:(tt + 1) * D])
                    ins.then_inc(S("sst").h, 16)
                s.wait_ge(S("sst").h, stgt)
            em("sync", spill)

            def vstat(v, tgt=att_done):
                v.wait_ge(S("vca").h, tgt)
                src3 = comb_sb[:, :].rearrange("p (t d) -> p t d", t=4)
                for h in range(NH):
                    ins = v.reduce_sum(stattmp[:, h:h + 1],
                                       src3[:, :, h * HE:(h + 1) * HE], axis=AX.XY)
                ins.then_inc(S("vst").h, 1)
            S("vst").n += 1
            em("vector", vstat)

            def sstat(sc, tgt=att_done):
                sc.wait_ge(S("vca").h, tgt)
                sc.drain()
                src3 = comb_sb[:, :].rearrange("p (t d) -> p t d", t=4)
                for h in range(NH):
                    if h >= 2:
                        sc.drain()
                    ins = sc.activation(sqdump[:, (h % 2) * 256:(h % 2) * 256 + 256],
                                        src3[:, :, h * HE:(h + 1) * HE],
                                        AF.Square, accum_out=stattmp[:, 16 + h:17 + h])
                ins.then_inc(S("scst").h, 1)
            S("scst").n += 1
            em("scalar", sstat)

            def vacc(v, tgt=S("scst").n, tgt2=S("vst").n):
                v.wait_ge(S("scst").h, tgt)
                v.wait_ge(S("vst").h, tgt2)
                v.drain()
                ins = v.tensor_add(statm[:, :], statm[:, :], stattmp[:, :])
                ins.then_inc(S("vst").h, 1)
            S("vst").n += 1
            em("vector", vacc)


        def gn_t(t, tgt=S("vst").n):
            t.wait_ge(S("vst").h, tgt)
            ins = t.matmul(bank(2)[0:1, 0:32], onecol[:, :], statm[:, :],
                           start=True, stop=True)
            ins.then_inc(S("pgn").h, 1)
        S("pgn").n += 1
        em("tensor", gn_t)

        def gn_v1(v, tgt=S("pgn").n):
            v.wait_ge(S("pgn").h, tgt)
            v.tensor_scalar_mul(srow[:, :], bank(2)[0:1, 0:32], 1.0 / GNN)
            v.drain()
            v.tensor_mul(srow2[:, :], srow[0:1, 0:16], srow[0:1, 0:16])
            v.drain()
            ins = v.tensor_sub(srow2[:, :], srow[0:1, 16:32], srow2[:, :])
            ins.then_inc(S("vgn").h, 1)
        S("vgn").n += 1
        em("vector", gn_v1)

        def gn_s(sc, tgt=S("vgn").n):
            sc.wait_ge(S("vgn").h, tgt)
            sc.drain()
            sc.activation(srow2[:, :], srow2[:, :], AF.Ln, bias=cst[0:1, 0:1])
            sc.drain()
            ins = sc.activation(srow[0:1, 16:32], srow2[:, :], AF.Exp, scale=cst[0:1, 1:2])
            ins.then_inc(S("scg2").h, 1)
        S("scg2").n += 1
        em("scalar", gn_s)

        def gn_t2(t, tgt=S("scg2").n):
            t.wait_ge(S("scg2").h, tgt)
            ins = t.matmul(bank(2)[:, 0:32], onerow[:, :], srow[:, :],
                           start=True, stop=True)
            ins.then_inc(S("pgn").h, 1)
        S("pgn").n += 1
        em("tensor", gn_t2)

        def gn_v3(v, tgt=S("pgn").n):
            v.wait_ge(S("pgn").h, tgt)
            v.tensor_copy(bcast[:, :], bank(2)[:, 0:32])
            v.drain()
            for half in range(2):
                rows = slice(half * HE, (half + 1) * HE)
                v.tensor_mul(s12[rows, 0:8], bcast[rows, 16 + half:32:2], gnwb[rows, 0:8])
            v.drain()
            for half in range(2):
                rows = slice(half * HE, (half + 1) * HE)
                v.tensor_mul(s12[rows, 8:16], bcast[rows, half:16:2], s12[rows, 0:8])
            v.drain()
            ins = v.tensor_sub(s12[:, 8:16], gnwb[:, 8:16], s12[:, 8:16])
            ins.then_inc(S("vgn").h, 1)
        S("vgn").n += 1
        em("vector", gn_v3)
        gn_done = S("vgn").n
        if p1_att and p1_qkv and p1_gn and p1_stats:
            em.flush_block()
        else:
            em.q = {k: [] for k in ENGINES}

    if (2 not in phases) or not (p1_att and p1_qkv and p1_gn and p1_stats):
        es.close()
        return nc

    with contextlib.ExitStack() as p2es:
        sb2 = lambda name, shape, dt: p2es.enter_context(nc.sbuf_tensor(name, shape, dt))
        ps2 = lambda name, shape, dt: p2es.enter_context(nc.psum_tensor(name, shape, dt))

        combt_sb = sb2("combt_sb", [C, 4 * D], BF16)
        combT = sb2("combT", [C, 8 * SC], BF16)
        x1_sb = sb2("x1_sb", [C, 2 * 4 * D], F32)
        xln2_bf = sb2("xln2_bf", [C, 4 * D], BF16)
        xln2T = sb2("xln2T", [C, 8 * SC], BF16)
        wring = sb2("wring", [C, 3 * WSLOT], BF16)
        h_sb = sb2("h_sb", [C, 2 * NIT * SC], BF16)
        silu_sb = sb2("silu_sb", [C, 2 * SC], BF16)
        ucp_sb = sb2("ucp_sb", [C, 2 * SC], BF16)
        el_sb = sb2("el_sb", [C, 8], F32)
        cmask = sb2("cmask", [C, 8], F32)
        c8_sb = sb2("c8_sb", [C, 4 * NEXP], F32)
        rcol = sb2("rcol", [C, 8], F32)

        psf2 = ps2("psf2", [C, 6 * 512], F32)
        psb2 = ps2("psb2", [C, 2048], BF16)
        bank2 = lambda i: psf2[:, i * 512:(i + 1) * 512]

        def conv_src2(p):
            e, j = p // 3, p % 3
            if e < 8:
                hs = (wg_h, wu_h, wd_h)[j]
                off = e * D * IFF
            else:
                hs = (swg_h, swu_h, swd_h)[j]
                off = 0
            if j < 2:
                return bass.AP(hs, off, [[IFF, C], [C * IFF, 8], [1, IFF]])
            return bass.AP(hs, off, [[D, C], [C * D, NIT], [1, D]])

        piece_done = {}
        piece_tgt = {}

        def emit_piece_load(p):
            wait = piece_done.get(p - 3)
            sname = "plds%d" % (p % 3)
            S(sname).n += 16
            piece_tgt[p] = (sname, S(sname).n)

            def gload(g, p=p, wait=wait, sname=sname, first=(p == 0),
                      cv=(S("gcs0").n, S("gcs1").n)):
                if first and use_conv:
                    g.wait_ge(S("gcs0").h, cv[0])
                    g.wait_ge(S("gcs1").h, cv[1])
                if wait is not None:
                    g.wait_ge(S(wait[0]).h, wait[1])
                ins = g.dma_start(out=wring[:, (p % 3) * WSLOT:(p % 3) * WSLOT + WSLOT],
                                  in_=(wbf_h[p % PPT] if use_conv else conv_src2(p % PPT)))
                ins.then_inc(S(sname).h, 16)
            em("gpsimd", gload)

        for p in range(3):
            emit_piece_load(p)
        next_piece = 3

        def emit_t512b(src_t, dst, scale_cols=None, shift_cols=None):
            pe, vc = S("pet"), S("vct")
            for dt in range(8):
                reg = psb2[:, (dt % 2) * 1024:(dt % 2) * 1024 + 512]

                def ttr(t, dt=dt, reg=reg, banktgt=vc.n - 1):
                    if banktgt >= 1:
                        t.wait_ge(vc.h, banktgt)
                    for tt in range(4):
                        ins = t.transpose(reg[:, tt * C:(tt + 1) * C],
                                          src_t[:, tt * D + dt * C:tt * D + (dt + 1) * C],
                                          eye[:, :])
                    ins.then_inc(pe.h, 1)
                pe.n += 1
                em("tensor", ttr)

                def tev(v, dt=dt, reg=reg, tgt=pe.n):
                    v.wait_ge(pe.h, tgt)
                    if scale_cols is not None:
                        ins = v.tensor_scalar(dst[:, dt * SC:(dt + 1) * SC], reg[:, :],
                                              scale_cols[:, dt:dt + 1], shift_cols[:, dt:dt + 1],
                                              ALU.mult, ALU.add)
                    else:
                        ins = v.tensor_copy(dst[:, dt * SC:(dt + 1) * SC], reg[:, :])
                    ins.then_inc(vc.h, 1)
                vc.n += 1
                em("vector", tev)

        store_tgts = []

        for t_i in range(NTILE):
            x1 = x1_sb[:, (t_i % 2) * 4 * D:(t_i % 2 + 1) * 4 * D]

            if t_i == 0:
                S("sld").n += 128

                def ld0(s, tgt=S("sld").n):
                    for tt in range(4):
                        s.dma_start(out=combt_sb[:, tt * D:(tt + 1) * D],
                                    in_=comb_h[tt * C:(tt + 1) * C, :]).then_inc(S("sld").h, 16)
                        ins = s.dma_start(out=x_sb[:, tt * D:(tt + 1) * D],
                                          in_=x_h[tt * C:(tt + 1) * C, :])
                        ins.then_inc(S("sld").h, 16)
                    s.wait_ge(S("sld").h, tgt)
                em("sync", ld0)
            combx_tgt = S("sld").n

            def twx2(t, tgt=combx_tgt, g=gn_done, first=(t_i == 0)):
                t.wait_ge(S("sld").h, tgt)
                if first:
                    t.wait_ge(S("vgn").h, g)
            em("tensor", twx2)

            em("vector", lambda v, tgt=combx_tgt: v.wait_ge(S("sld").h, tgt))
            em("scalar", lambda sc, tgt=combx_tgt: (sc.wait_ge(S("sld").h, tgt), sc.drain()))

            emit_t512b(combt_sb, combT, s12[:, 0:8], s12[:, 8:16])
            combT_done = S("vct").n
            pet_tr_done = S("pet").n

            if t_i + 1 < NTILE:
                S("sld").n += 64

                def ldc(s, t_n=t_i + 1, ptgt=pet_tr_done, tgt=S("sld").n):
                    s.wait_ge(S("pet").h, ptgt)
                    for tt in range(4):
                        ins = s.dma_start(out=combt_sb[:, tt * D:(tt + 1) * D],
                                          in_=comb_h[t_n * SC + tt * C:t_n * SC + (tt + 1) * C, :])
                        ins.then_inc(S("sld").h, 16)
                    s.wait_ge(S("sld").h, tgt)
                em("sync", ldc)

            vcw_tloc = []
            for tloc in range(4):
                for dc in range(2):
                    gidx = tloc * 2 + dc
                    bk = bank2(gidx % 2)

                    def twout(t, tloc=tloc, dc=dc, bk=bk, gidx=gidx,
                              ctgt=combT_done, banktgt=S("vcw").n - 1,
                              dtgt=S("vcd").n):
                        if gidx == 0:
                            t.wait_ge(S("vct").h, ctgt)
                            if dtgt >= 1:
                                t.wait_ge(S("vcd").h, dtgt)
                        if banktgt >= 1:
                            t.wait_ge(S("vcw").h, banktgt)
                        for dt in range(8):
                            ins = t.matmul(bk,
                                           combT[:, dt * SC + tloc * C:dt * SC + (tloc + 1) * C],
                                           wout_bf[:, dt * D + dc * 512:dt * D + (dc + 1) * 512],
                                           start=(dt == 0), stop=(dt == 7))
                        ins.then_inc(S("pew").h, 1)
                    S("pew").n += 1
                    em("tensor", twout)

                    def vwout(v, tloc=tloc, dc=dc, bk=bk, tgt=S("pew").n, x1=x1,
                              sttgt=(store_tgts[t_i - 2] if (gidx == 0 and t_i >= 2) else None)):
                        v.wait_ge(S("pew").h, tgt)
                        if sttgt is not None:
                            v.wait_ge(S(sttgt[0]).h, sttgt[1])
                        ins = v.tensor_add(x1[:, tloc * D + dc * 512:tloc * D + (dc + 1) * 512],
                                           x_sb[:, tloc * D + dc * 512:tloc * D + (dc + 1) * 512],
                                           bk)
                        ins.then_inc(S("vcw").h, 1)
                    S("vcw").n += 1
                    em("vector", vwout)
                vcw_tloc.append(("vcw", S("vcw").n))
            wout_done = S("vcw").n

            if t_i + 1 < NTILE:
                S("sld").n += 64

                def ldx2(s, t_n=t_i + 1, wtgt=wout_done, tgt=S("sld").n):
                    s.wait_ge(S("vcw").h, wtgt)
                    for tt in range(4):
                        ins = s.dma_start(out=x_sb[:, tt * D:(tt + 1) * D],
                                          in_=x_h[t_n * SC + tt * C:t_n * SC + (tt + 1) * C, :])
                        ins.then_inc(S("sld").h, 16)
                    s.wait_ge(S("sld").h, tgt)
                em("sync", ldx2)

            ln2_done = emit_ln_sc(x1, xln2_bf, per_tt_wait=vcw_tloc)

            em("tensor", lambda t, tgt=ln2_done: t.wait_ge(S("sc1").h, tgt))
            emit_t512b(xln2_bf, xln2T)
            xln2T_done = S("vct").n

            rreg = [(0, 0), (1, 0), (0, 16), (1, 16)]
            rd_tgts = {}

            def emit_trt(tloc, wait_scr=None):
                rb, rc0 = rreg[tloc]

                def trt(t, tloc=tloc, rb=rb, rc0=rc0, ctgt=xln2T_done,
                        wtgt=wout_done, wait_scr=wait_scr):
                    if tloc == 0:
                        t.wait_ge(S("vct").h, ctgt)
                        t.wait_ge(S("vcw").h, wtgt)
                    if wait_scr is not None:
                        t.wait_ge(S("scr").h, wait_scr)
                    for dt in range(8):
                        ins = t.matmul(bank2(rb)[:, rc0:rc0 + 9],
                                       xln2T[:, dt * SC + tloc * C:dt * SC + (tloc + 1) * C],
                                       gate9_bf[:, dt * 9:(dt + 1) * 9],
                                       start=(dt == 0), stop=(dt == 7))
                    ins.then_inc(S("per").h, 1)
                S("per").n += 1
                em("tensor", trt)

            emit_trt(0)
            emit_trt(1)
            for tloc in range(4):
                rb, rc0 = rreg[tloc]

                def vr1(v, rb=rb, rc0=rc0, tgt=(S("per").n - 1 + (tloc if tloc < 2 else 1))):
                    v.wait_ge(S("per").h, tgt)
                    ins = v.reduce_max(rcol[:, 0:1], bank2(rb)[:, rc0:rc0 + 8], axis=AX.X,
                                       negate=True)
                    ins.then_inc(S("vcr").h, 1)
                S("vcr").n += 1
                em("vector", vr1)

                def sr1(sc, rb=rb, rc0=rc0, tgt=S("vcr").n):
                    sc.wait_ge(S("vcr").h, tgt)
                    sc.activation(el_sb[:, 0:8], bank2(rb)[:, rc0:rc0 + 8], AF.Exp,
                                  bias=rcol[:, 0:1])
                    ins = sc.activation(rcol[:, 1:2], bank2(rb)[:, rc0 + 8:rc0 + 9], AF.Exp,
                                        scale=cst[:, 2:3])
                    ins.then_inc(S("scr").h, 1)
                S("scr").n += 1
                em("scalar", sr1)
                rd_tgts[tloc] = S("scr").n

                def vr2(v, tloc=tloc, tgt=S("scr").n):
                    v.wait_ge(S("scr").h, tgt)
                    v.drain()
                    v.tensor_scalar_add(rcol[:, 2:3], rcol[:, 1:2], 1.0)
                    v.drain()
                    v.reciprocal(c8_sb[:, tloc * NEXP + 8:tloc * NEXP + 9], rcol[:, 2:3])
                    v.reduce_max(rcol[:, 3:4], el_sb[:, 0:8], axis=AX.X)
                    v.drain()
                    v.tensor_scalar(cmask[:, :], el_sb[:, 0:8], rcol[:, 3:4], None, ALU.is_ge)
                    v.drain()
                    v.scalar_tensor_tensor(cmask[:, :], cmask[:, :], -1000.0, el_sb[:, 0:8],
                                           ALU.mult, ALU.add)
                    v.drain()
                    v.reduce_max(rcol[:, 4:5], cmask[:, :], axis=AX.X)
                    v.drain()
                    v.tensor_add(rcol[:, 5:6], rcol[:, 3:4], rcol[:, 4:5])
                    v.drain()
                    v.reciprocal(rcol[:, 6:7], rcol[:, 5:6])
                    v.tensor_scalar(cmask[:, :], el_sb[:, 0:8], rcol[:, 4:5], None, ALU.is_ge)
                    v.drain()
                    ins = v.scalar_tensor_tensor(c8_sb[:, tloc * NEXP:tloc * NEXP + 8],
                                                 el_sb[:, 0:8], rcol[:, 6:7], cmask[:, :],
                                                 ALU.mult, ALU.mult)
                    ins.then_inc(S("vcr").h, 1)
                S("vcr").n += 1
                em("vector", vr2)
                if tloc < 2:
                    emit_trt(tloc + 2, wait_scr=rd_tgts[tloc])
            router_done = S("vcr").n
            router_sc_done = S("scr").n

            for e in range(NEXP):
                p0 = PPT * t_i + 3 * e
                hb = (e % 2) * NIT * SC
                for it in range(NIT):
                    gb = bank2(2 + (it % 2) * 2)
                    ub = bank2(3 + (it % 2) * 2)
                    wg0 = (p0 % 3) * WSLOT
                    wu0 = ((p0 + 1) % 3) * WSLOT

                    def tgu(t, it=it, gb=gb, ub=ub, wg0=wg0, wu0=wu0,
                            ldg=piece_tgt[p0], ldu=piece_tgt[p0 + 1],
                            banktgt=S("scg").n - 1):
                        if it == 0:
                            t.wait_ge(S(ldg[0]).h, ldg[1])
                            t.wait_ge(S(ldu[0]).h, ldu[1])
                        if banktgt >= 1:
                            t.wait_ge(S("scg").h, banktgt)
                        for dt in range(8):
                            t.matmul(gb, wring[:, wg0 + dt * IFF + it * C:wg0 + dt * IFF + (it + 1) * C],
                                     xln2T[:, dt * SC:(dt + 1) * SC],
                                     start=(dt == 0), stop=(dt == 7))
                        for dt in range(8):
                            ins = t.matmul(ub, wring[:, wu0 + dt * IFF + it * C:wu0 + dt * IFF + (it + 1) * C],
                                           xln2T[:, dt * SC:(dt + 1) * SC],
                                           start=(dt == 0), stop=(dt == 7))
                        ins.then_inc(S("pe2").h, 1)
                    S("pe2").n += 1
                    em("tensor", tgu)

                    def sgu(sc, it=it, gb=gb, ub=ub, tgt=S("pe2").n,
                            vtgt=S("vch").n - 1):
                        sc.wait_ge(S("pe2").h, tgt)
                        if vtgt >= 1:
                            sc.wait_ge(S("vch").h, vtgt)
                        sc.activation(silu_sb[:, (it % 2) * SC:(it % 2 + 1) * SC], gb, AF.Silu)
                        ins = sc.copy(ucp_sb[:, (it % 2) * SC:(it % 2 + 1) * SC], ub)
                        ins.then_inc(S("scg").h, 1)
                    S("scg").n += 1
                    em("scalar", sgu)

                    def vh(v, it=it, tgt=S("scg").n, hb=hb, dtgt=max(0, S("ped").n - 8)):
                        v.wait_ge(S("scg").h, tgt)
                        if it == 0 and dtgt >= 1:
                            v.wait_ge(S("ped").h, dtgt)
                        ins = v.tensor_mul(h_sb[:, hb + it * SC:hb + (it + 1) * SC],
                                           silu_sb[:, (it % 2) * SC:(it % 2 + 1) * SC],
                                           ucp_sb[:, (it % 2) * SC:(it % 2 + 1) * SC])
                        ins.then_inc(S("vch").h, 1)
                    S("vch").n += 1
                    em("vector", vh)
                h_done = S("vch").n
                piece_done[p0] = ("scg", S("scg").n)
                piece_done[p0 + 1] = ("scg", S("scg").n)
                while next_piece < NPIECES and (next_piece - 3) in piece_done:
                    emit_piece_load(next_piece)
                    next_piece += 1

                for tloc in range(4):
                    for dc in range(2):
                        didx = tloc * 2 + dc
                        db = bank2(didx % 2)
                        wd0 = ((p0 + 2) % 3) * WSLOT

                        def tdown(t, tloc=tloc, dc=dc, db=db, wd0=wd0, didx=didx, e=e,
                                  htgt=h_done, ldd=piece_tgt[p0 + 2], hb=hb,
                                  banktgt=(S("vcd").n - 1 if (e > 0 or didx >= 2) else -1),
                                  sctgt=router_sc_done):
                            if didx == 0:
                                t.wait_ge(S("vch").h, htgt)
                                t.wait_ge(S(ldd[0]).h, ldd[1])
                                if e == 0:
                                    t.wait_ge(S("scr").h, sctgt)
                            if banktgt >= 1:
                                t.wait_ge(S("vcd").h, banktgt)
                            for it in range(NIT):
                                ins = t.matmul(db,
                                               h_sb[:, hb + it * SC + tloc * C:hb + it * SC + (tloc + 1) * C],
                                               wring[:, wd0 + it * D + dc * 512:wd0 + it * D + (dc + 1) * 512],
                                               start=(it == 0), stop=(it == NIT - 1))
                            ins.then_inc(S("ped").h, 1)
                        S("ped").n += 1
                        em("tensor", tdown)

                        def vdown(v, tloc=tloc, dc=dc, db=db, e=e, didx=didx,
                                  tgt=S("ped").n, lntgt=ln2_done, x1=x1,
                                  rtgt=router_done):
                            v.wait_ge(S("ped").h, tgt)
                            if e == 0 and didx == 0:
                                v.wait_ge(S("sc1").h, lntgt)
                                v.wait_ge(S("vcr").h, rtgt)
                            ccol = c8_sb[:, tloc * NEXP + min(e, 8):tloc * NEXP + min(e, 8) + 1]
                            ins = v.scalar_tensor_tensor(
                                x1[:, tloc * D + dc * 512:tloc * D + (dc + 1) * 512],
                                db, ccol,
                                x1[:, tloc * D + dc * 512:tloc * D + (dc + 1) * 512],
                                ALU.mult, ALU.add)
                            ins.then_inc(S("vcd").h, 1)
                        S("vcd").n += 1
                        em("vector", vdown)
                piece_done[p0 + 2] = ("ped", S("ped").n)
                while next_piece < NPIECES and (next_piece - 3) in piece_done:
                    emit_piece_load(next_piece)
                    next_piece += 1

            tile_done = S("vcd").n
            ssem = "sstA" if t_i % 2 == 0 else "sstB"
            S(ssem).n += 64
            store_tgts.append((ssem, S(ssem).n))

            def st(s, t_i=t_i, tgt=tile_done, ssem=ssem,
                   fin=(S("sstA").n, S("sstB").n), x1=x1):
                s.wait_ge(S("vcd").h, tgt)
                for tt in range(4):
                    ins = s.dma_start(out=out_h[t_i * SC + tt * C:t_i * SC + (tt + 1) * C, :],
                                      in_=x1[:, tt * D:(tt + 1) * D])
                    ins.then_inc(S(ssem).h, 16)
                if t_i == NTILE - 1:
                    s.wait_ge(S("sstA").h, fin[0])
                    s.wait_ge(S("sstB").h, fin[1])
            em("sync", st)

        em.flush_block()

    es.close()
    return nc
def _make_consts():
    g64 = _gammas64()
    iota = np.arange(C, dtype=np.float64)
    diff = iota[:, None] - iota[None, :]
    dmask = np.zeros((C, NH * C), dtype=np.float32)
    gcols = np.zeros((C, 2 * NH), dtype=np.float32)
    for h in range(NH):
        g = g64[h]
        Dm = np.where(diff >= 0, g ** np.maximum(diff, 0) / 8.0, 0.0)
        dmask[:, h * C:(h + 1) * C] = Dm.T.astype(np.float32)
        gcols[:, h] = (g ** iota / 8.0).astype(np.float32)
        gcols[:, NH + h] = (g ** (C - iota)).astype(np.float32)
    eye = np.eye(C, dtype=ml_dtypes.bfloat16)
    return dmask, gcols, eye


def kernel(**inputs):
    f32 = lambda k: np.ascontiguousarray(np.asarray(inputs[k], dtype=np.float32))
    x = f32("L0_attn_x")
    wq = np.ascontiguousarray(np.transpose(f32("L0_attn_Wq"), (1, 0, 2)).reshape(D, D))
    wk = np.ascontiguousarray(np.transpose(f32("L0_attn_Wk"), (1, 0, 2)).reshape(D, D))
    wv = np.ascontiguousarray(np.transpose(f32("L0_attn_Wv"), (1, 0, 2)).reshape(D, D))
    wout = f32("L0_attn_Wout")
    gate9 = np.ascontiguousarray(np.concatenate(
        [f32("L0_ffn_gate_w").T, f32("L0_ffn_shared_gate_w").T], axis=1))
    gnw = f32("L0_attn_gn_weight").reshape(8, C).T
    gnb = f32("L0_attn_gn_bias").reshape(8, C).T
    gnwb = np.ascontiguousarray(np.concatenate([gnw, gnb], axis=1))
    dmask, gcols, eye = _make_consts()

    nc = build_nc()
    shared = dict(wq=wq, wk=wk, wv=wv, wout=wout, gate9=gate9, gnwb=gnwb,
                  wg=f32("L0_ffn_w_gate"), wu=f32("L0_ffn_w_up"), wd=f32("L0_ffn_w_down"),
                  swg=f32("L0_ffn_sw_gate"), swu=f32("L0_ffn_sw_up"),
                  swd=f32("L0_ffn_sw_down"),
                  dmask=dmask, gcols=gcols, eye=eye)
    in_maps = [dict(shared, x=np.ascontiguousarray(x[b])) for b in range(8)]
    res = run_bass_kernel_spmd(nc, in_maps, core_ids=list(range(8)))
    return np.stack([np.asarray(r["out"], dtype=np.float32) for r in res.results], axis=0)


# version 05
# speedup vs baseline: 1.0684x; 1.0684x over previous
import math
import contextlib

import numpy as np
import ml_dtypes

import concourse.bass as bass
import concourse.mybir as mybir
from concourse.bass_utils import run_bass_kernel_spmd

F32 = mybir.dt.float32
BF16 = mybir.dt.bfloat16
AF = mybir.ActivationFunctionType
ALU = mybir.AluOpType
AX = mybir.AxisListType

L, D, NH, HE = 4096, 1024, 16, 64
C = 128
SC = 512
NSC = L // SC
NCC = SC // C
IFF = 1408
NIT = IFF // 128
NEXP = 9
NTILE = L // SC
EPS = 1e-5
GNN = float(HE * L)
ENGINES = ("tensor", "vector", "scalar", "sync", "gpsimd")
PPT = 3 * NEXP
NPIECES = PPT * NTILE
WSLOT = 8 * IFF


def _gammas64():
    return np.array([1.0 - 2.0 ** (-5 - math.floor(8 * i / NH)) for i in range(NH)],
                    dtype=np.float64)


class Sem:
    def __init__(self, nc, es, name):
        self.h = es.enter_context(nc.semaphore(name))
        self.n = 0


class Emitter:
    def __init__(self, nc):
        self.nc = nc
        self.q = {k: [] for k in ENGINES}

    def __call__(self, eng, fn):
        self.q[eng].append(fn)

    def flush_block(self):
        with self.nc.Block() as block:
            for name in ENGINES:
                fns = self.q[name]
                if fns:
                    getattr(block, name)(lambda e, fns=tuple(fns): [f(e) for f in fns])
        self.q = {k: [] for k in ENGINES}


def build_nc(dbg=False, phases=(1, 2), nsc=None, p1_att=True, p1_qkv=True, p1_gn=True, p1_stats=True,
             use_conv=True):
    if nsc is None:
        nsc = NSC
    nc = bass.Bass("TRN2", target_bir_lowering=False)

    x_h = nc.dram_tensor("x", [L, D], F32, kind="ExternalInput").ap()
    wq_h = nc.dram_tensor("wq", [D, D], F32, kind="ExternalInput")
    wk_h = nc.dram_tensor("wk", [D, D], F32, kind="ExternalInput")
    wv_h = nc.dram_tensor("wv", [D, D], F32, kind="ExternalInput")
    wout_h = nc.dram_tensor("wout", [D, D], F32, kind="ExternalInput")
    gate9_h = nc.dram_tensor("gate9", [D, 9], F32, kind="ExternalInput")
    wg_h = nc.dram_tensor("wg", [8, D, IFF], F32, kind="ExternalInput")
    wu_h = nc.dram_tensor("wu", [8, D, IFF], F32, kind="ExternalInput")
    wd_h = nc.dram_tensor("wd", [8, IFF, D], F32, kind="ExternalInput")
    swg_h = nc.dram_tensor("swg", [D, IFF], F32, kind="ExternalInput")
    swu_h = nc.dram_tensor("swu", [D, IFF], F32, kind="ExternalInput")
    swd_h = nc.dram_tensor("swd", [IFF, D], F32, kind="ExternalInput")
    dmask_h = nc.dram_tensor("dmask", [C, NH * C], F32, kind="ExternalInput").ap()
    gcols_h = nc.dram_tensor("gcols", [C, 2 * NH], F32, kind="ExternalInput").ap()
    gnwb_h = nc.dram_tensor("gnwb", [C, 16], F32, kind="ExternalInput").ap()
    eye_h = nc.dram_tensor("eye", [C, C], BF16, kind="ExternalInput").ap()
    out_h = nc.dram_tensor("out", [L, D], F32, kind="ExternalOutput").ap()
    comb_h = nc.dram_tensor("comb", [L, D], BF16,
                            kind=("ExternalOutput" if dbg else "Internal")).ap()
    wbf_h = nc.dram_tensor("wbf", [PPT, C, WSLOT], BF16, kind="Internal").ap()

    g64 = _gammas64()
    gC_imm = [float(np.float32(g64[h] ** C)) for h in range(NH)]

    es = contextlib.ExitStack()
    sb = lambda name, shape, dt: es.enter_context(nc.sbuf_tensor(name, shape, dt))

    eye = sb("eye_sb", [C, C], BF16)
    gnwb = sb("gnwb_sb", [C, 16], F32)
    wout_bf = sb("wout_bf", [C, 8 * D], BF16)
    gate9_bf = sb("gate9_bf", [C, 8 * 9], BF16)
    x_sb = sb("x_sb", [C, 4 * D], F32)
    cols = sb("cols", [C, 32], F32)
    sqdump = sb("sqdump", [C, 2 * D], BF16)
    statm = sb("statm", [C, 32], F32)
    srow = sb("srow", [1, 32], F32)
    srow2 = sb("srow2", [1, 16], F32)
    bcast = sb("bcast", [C, 32], F32)
    s12 = sb("s12", [C, 16], F32)
    onecol = sb("onecol", [C, 1], F32)
    onerow = sb("onerow", [1, C], F32)
    cst = sb("cst", [C, 3], F32)

    sems = {}
    for name in ("sld", "gld", "sst", "vc1", "sc1", "pet", "vct", "peq", "vcq",
                 "pea", "vca", "vst", "scst", "pgn", "vgn", "scg2",
                 "pew", "vcw", "per", "vcr", "scr", "pe2", "scg", "vch",
                 "ped", "vcd", "plds0", "plds1", "plds2",
                 "gcv0", "gcv1", "gcs0", "gcs1", "sstA", "sstB",
                 "scku", "sctb", "scq"):
        sems[name] = Sem(nc, es, name)
    S = lambda k: sems[k]

    em = Emitter(nc)

    def emit_ln_sc(src, dst, per_tt_wait=None):
        ss = S("sc1")

        def s_ln(sc):
            for tt in range(4):
                if per_tt_wait is not None:
                    w = per_tt_wait[tt]
                    sc.wait_ge(S(w[0]).h, w[1])
                if tt >= 2:
                    sc.drain()
                sc.activation(sqdump[:, (tt % 2) * D:(tt % 2 + 1) * D],
                              src[:, tt * D:(tt + 1) * D], AF.Identity,
                              accum_out=cols[:, tt:tt + 1])
                sc.drain()
                sc.activation(sqdump[:, (tt % 2) * D:(tt % 2 + 1) * D],
                              src[:, tt * D:(tt + 1) * D], AF.Square,
                              accum_out=cols[:, 4 + tt:5 + tt])
            sc.drain()
            sc.mul(cols[:, 8:12], cols[:, 0:4], 1.0 / D)
            sc.drain()
            for tt in range(4):
                sc.mul(cols[:, 16 + tt:17 + tt], cols[:, 8 + tt:9 + tt],
                       cols[:, 8 + tt:9 + tt])
            sc.drain()
            sc.mul(cols[:, 16:20], cols[:, 16:20], -1.0)
            sc.mul(cols[:, 20:24], cols[:, 4:8], 1.0 / D)
            sc.drain()
            for tt in range(4):
                sc.add(cols[:, 20 + tt:21 + tt], cols[:, 20 + tt:21 + tt],
                       cols[:, 16 + tt:17 + tt])
            sc.drain()
            sc.activation(cols[:, 16:20], cols[:, 20:24], AF.Ln, bias=cst[:, 0:1])
            sc.drain()
            sc.activation(cols[:, 12:16], cols[:, 16:20], AF.Exp, scale=cst[:, 1:2])
            sc.mul(cols[:, 24:28], cols[:, 8:12], -1.0)
            sc.drain()
            for tt in range(4):
                sc.mul(cols[:, 24 + tt:25 + tt], cols[:, 24 + tt:25 + tt],
                       cols[:, 12 + tt:13 + tt])
            sc.drain()
            for tt in range(4):
                ins = sc.activation(dst[:, tt * D:(tt + 1) * D],
                                    src[:, tt * D:(tt + 1) * D], AF.Identity,
                                    scale=cols[:, 12 + tt:13 + tt],
                                    bias=cols[:, 24 + tt:25 + tt])
            ins.then_inc(ss.h, 1)
        ss.n += 1
        em("scalar", s_ln)
        return ss.n

    with contextlib.ExitStack() as p1es:
        sb1 = lambda name, shape, dt: p1es.enter_context(nc.sbuf_tensor(name, shape, dt))
        ps1 = lambda name, shape, dt: p1es.enter_context(nc.psum_tensor(name, shape, dt))

        dmask = sb1("dmask_sb", [C, NH * C], F32)
        gcols = sb1("gcols_sb", [C, 2 * NH], F32)
        wq_bf = sb1("wq_bf", [C, 8 * D], BF16)
        wk_bf = sb1("wk_bf", [C, 8 * D], BF16)
        wv_bf = sb1("wv_bf", [C, 8 * D], BF16)
        xln_bf = sb1("xln_bf", [C, 4 * D], BF16)
        xlnT = sb1("xlnT", [C, 8 * SC], BF16)
        qt_sb = sb1("qt_sb", [C, 8 * SC], BF16)
        kt_sb = sb1("kt_sb", [C, 8 * SC], BF16)
        qt2_sb = sb1("qt2_sb", [HE, 8 * SC], BF16)
        kt2_sb = sb1("kt2_sb", [HE, 8 * SC], BF16)
        v_sb = sb1("v_sb", [C, 4 * D], BF16)
        comb_sb = sb1("comb_sb", [C, 4 * D], BF16)
        tf32 = sb1("tf32", [HE, NH * HE], F32)
        tbf = sb1("tbf", [HE, NH * HE], BF16)
        am_sb = sb1("am_sb", [C, 4 * C], BF16)
        ku_sb = sb1("ku_sb", [C, 2 * 4 * HE], BF16)
        oicp = sb1("oicp", [C, 2 * 4 * HE], F32)
        stattmp = sb1("stattmp", [C, 32], F32)
        stg = sb1("stg", [C, WSLOT], BF16)

        psf = ps1("psf", [C, 6 * 512], F32)
        psb = ps1("psb", [C, 2048], BF16)

        bank = lambda i: psf[:, i * 512:(i + 1) * 512]

        def tsl(h):
            return slice(0, HE), slice(h * HE, (h + 1) * HE)

        def hsl(t_main, t_alt, h, ccx):
            src_t = t_main if h % 2 == 0 else t_alt
            return src_t[0:HE, (h // 2) * SC + ccx * C:(h // 2) * SC + (ccx + 1) * C]

        S("sld").n += 64

        def p0_sync(s, tgt=S("sld").n):
            for dst, src in ((dmask, dmask_h), (gcols, gcols_h), (gnwb, gnwb_h), (eye, eye_h)):
                ins = s.dma_start(out=dst[:, :], in_=src)
                ins.then_inc(S("sld").h, 16)
            s.wait_ge(S("sld").h, tgt)
        em("sync", p0_sync)

        S("gld").n += 80

        def p0_gp(g, tgt=S("gld").n):
            for dst, src in ((wq_bf, wq_h), (wk_bf, wk_h), (wv_bf, wv_h), (wout_bf, wout_h)):
                g.dma_start(out=dst[:, :],
                            in_=bass.AP(src, 0, [[D, C], [C * D, 8], [1, D]])
                            ).then_inc(S("gld").h, 16)
            g.dma_start(out=gate9_bf[:, :],
                        in_=bass.AP(gate9_h, 0, [[9, C], [C * 9, 8], [1, 9]])
                        ).then_inc(S("gld").h, 16)
            g.wait_ge(S("gld").h, tgt)
        em("gpsimd", p0_gp)

        def p0_vec(v):
            v.memset(tf32[:, :], 0.0)
            v.memset(tbf[:, :], 0.0)
            v.memset(statm[:, :], 0.0)
            v.memset(onecol[:, :], 1.0)
            v.memset(onerow[:, :], 1.0)
            v.memset(cst[:, 0:1], EPS)
            v.memset(cst[:, 1:2], -0.5)
            v.memset(cst[:, 2:3], -1.0)
        em("vector", p0_vec)
        em.flush_block()

        def emit_t512a(src, dst, wait_sem=None, wait_tgt=0):
            pe, vc = S("pet"), S("vct")
            for dt in range(8):
                reg = psb[:, (dt % 2) * 1024:(dt % 2) * 1024 + 512]

                def ttr(t, dt=dt, reg=reg, w=(wait_sem if dt == 0 else None),
                        wt=wait_tgt, banktgt=vc.n - 1):
                    if w is not None:
                        t.wait_ge(S(w).h, wt)
                    if banktgt >= 1:
                        t.wait_ge(vc.h, banktgt)
                    for tt in range(4):
                        ins = t.transpose(reg[:, tt * C:(tt + 1) * C],
                                          src[:, tt * D + dt * C:tt * D + (dt + 1) * C],
                                          eye[:, :])
                    ins.then_inc(pe.h, 1)
                pe.n += 1
                em("tensor", ttr)

                def tev(v, dt=dt, reg=reg, tgt=pe.n):
                    v.wait_ge(pe.h, tgt)
                    ins = v.tensor_copy(dst[:, dt * SC:(dt + 1) * SC], reg[:, :])
                    ins.then_inc(vc.h, 1)
                vc.n += 1
                em("vector", tev)

        def conv_src(p):
            e, j = p // 3, p % 3
            if e < 8:
                hs = (wg_h, wu_h, wd_h)[j]
                off = e * D * IFF
            else:
                hs = (swg_h, swu_h, swd_h)[j]
                off = 0
            if j < 2:
                return bass.AP(hs, off, [[IFF, C], [C * IFF, 8], [1, IFF]])
            return bass.AP(hs, off, [[D, C], [C * D, NIT], [1, D]])

        conv_state = {"p": 0}

        def emit_conversions(n):
            def gconv(g, plist=tuple(range(conv_state["p"], min(conv_state["p"] + n, PPT)))):
                for p in plist:
                    if p >= 1:
                        g.wait_ge(S("gcs0").h, 16 * p)
                    g.dma_start(out=stg[:, :], in_=conv_src(p)).then_inc(S("gcv0").h, 16)
                    g.wait_ge(S("gcv0").h, 16 * (p + 1))
                    g.dma_start(out=wbf_h[p], in_=stg[:, :]).then_inc(S("gcs0").h, 16)
            for p in range(conv_state["p"], min(conv_state["p"] + n, PPT)):
                S("gcv0").n += 16
                S("gcs0").n += 16
            conv_state["p"] = min(conv_state["p"] + n, PPT)
            em("gpsimd", gconv)

        am2 = sb1("am2", [C, 2 * 4 * C], BF16)
        xln2b = sb1("xln2b", [C, 4 * D], BF16)
        S("sld").n += 64

        def ldx0(s, tgt=S("sld").n):
            for tt in range(4):
                ins = s.dma_start(out=x_sb[:, tt * C * 0 + tt * D:(tt + 1) * D],
                                  in_=x_h[tt * C:(tt + 1) * C, :])
                ins.then_inc(S("sld").h, 16)
            s.wait_ge(S("sld").h, tgt)
        em("sync", ldx0)
        xtgt0 = S("sld").n
        em("scalar", lambda sc, tgt=xtgt0: (sc.wait_ge(S("sld").h, tgt), sc.drain()))
        ln_tgts = {0: emit_ln_sc(x_sb, xln_bf)}

        att_done_prev = 0
        spill_tgt_prev = 0
        att_state = {}
        xln_bufs = [xln_bf, xln2b]
        for c in range(nsc):
            if use_conv:
                emit_conversions(4)

            if c + 1 < nsc:
                S("sld").n += 64

                def ldx(s, c=c, vtgt=ln_tgts[c], tgt=S("sld").n):
                    s.wait_ge(S("sc1").h, vtgt)
                    for tt in range(4):
                        ins = s.dma_start(out=x_sb[:, tt * D:(tt + 1) * D],
                                          in_=x_h[(c + 1) * SC + tt * C:(c + 1) * SC + (tt + 1) * C, :])
                        ins.then_inc(S("sld").h, 16)
                    s.wait_ge(S("sld").h, tgt)
                em("sync", ldx)
                xtgt = S("sld").n
                em("scalar", lambda sc, tgt=xtgt: (sc.wait_ge(S("sld").h, tgt), sc.drain()))
                ln_tgts[c + 1] = emit_ln_sc(x_sb, xln_bufs[(c + 1) % 2])

            em("tensor", lambda t, tgt=att_done_prev: (t.wait_ge(S("vca").h, tgt)
                                                       if tgt else None))
            emit_t512a(xln_bufs[c % 2], xlnT, wait_sem="sc1", wait_tgt=ln_tgts[c])
            tr_done = S("vct").n

            groups = ([("q", i) for i in range(8)] + [("k", i) for i in range(8)]
                      + [("v", i) for i in range(8)])
            evac_tgts = []
            q_evac = {}
            k_evac = {}
            for gidx, (kind, idx) in enumerate(groups):
                bk = bank(gidx % 2)

                def tqkv(t, kind=kind, idx=idx, bk=bk, gidx=gidx,
                         trtgt=tr_done,
                         banktgt=(evac_tgts[gidx - 2] if gidx >= 2 else None),
                         atgt=att_done_prev):
                    if gidx == 0:
                        t.wait_ge(S("vct").h, trtgt)
                        if atgt:
                            t.wait_ge(S("vca").h, atgt)
                    if banktgt is not None:
                        t.wait_ge(S(banktgt[0]).h, banktgt[1])
                    for dt in range(8):
                        if kind == "q":
                            w = wq_bf[:, dt * D + idx * C:dt * D + (idx + 1) * C]
                            mv = xlnT[:, dt * SC:(dt + 1) * SC]
                        elif kind == "k":
                            w = wk_bf[:, dt * D + idx * C:dt * D + (idx + 1) * C]
                            mv = xlnT[:, dt * SC:(dt + 1) * SC]
                        else:
                            tt, ec = idx // 2, idx % 2
                            w = xlnT[:, dt * SC + tt * C:dt * SC + (tt + 1) * C]
                            mv = wv_bf[:, dt * D + ec * 512:dt * D + (ec + 1) * 512]
                        ins = t.matmul(bk, w, mv, start=(dt == 0), stop=(dt == 7))
                    ins.then_inc(S("peq").h, 1)
                S("peq").n += 1
                em("tensor", tqkv)

                if kind in ("q", "k"):
                    def vqkv(v, kind=kind, idx=idx, bk=bk, tgt=S("peq").n):
                        v.wait_ge(S("peq").h, tgt)
                        dst = (qt_sb if kind == "q" else kt_sb)[:, idx * SC:(idx + 1) * SC]
                        ins = v.tensor_copy(dst, bk)
                        ins.then_inc(S("vcq").h, 1)
                    S("vcq").n += 1
                    evac_tgts.append(("vcq", S("vcq").n))
                    if kind == "q":
                        q_evac[idx] = ("vcq", S("vcq").n)
                    else:
                        k_evac[idx] = ("vcq", S("vcq").n)
                    em("vector", vqkv)
                else:
                    def sqkv(sc, idx=idx, bk=bk, tgt=S("peq").n):
                        sc.wait_ge(S("peq").h, tgt)
                        tt, ec = idx // 2, idx % 2
                        dst = v_sb[:, tt * D + ec * 512:tt * D + (ec + 1) * 512]
                        ins = sc.copy(dst, bk)
                        ins.then_inc(S("scq").h, 1)
                    S("scq").n += 1
                    evac_tgts.append(("scq", S("scq").n))
                    em("scalar", sqkv)
            qkv_done_v = S("vcq").n
            qkv_done_s = S("scq").n

            for et in range(8):
                S("sld").n += 32

                def qk2(s, et=et, qw=q_evac[et], kw=k_evac[et], tgt=S("sld").n):
                    s.wait_ge(S(qw[0]).h, qw[1])
                    s.wait_ge(S(kw[0]).h, kw[1])
                    s.dma_start(out=qt2_sb[:, et * SC:(et + 1) * SC],
                                in_=qt_sb[HE:2 * HE, et * SC:(et + 1) * SC]).then_inc(S("sld").h, 16)
                    ins = s.dma_start(out=kt2_sb[:, et * SC:(et + 1) * SC],
                                      in_=kt_sb[HE:2 * HE, et * SC:(et + 1) * SC])
                    ins.then_inc(S("sld").h, 16)
                    s.wait_ge(S("sld").h, tgt)
                em("sync", qk2)
            qk2_done = S("sld").n

            abase = S("vca").n
            pea_p1 = {}
            pea_p2 = {}
            vca_v1 = {}
            vca_v2 = {}
            scku_m = {}
            sctb_m = att_state.setdefault("sctb_m", {})
            ab_iter = att_state.setdefault("ab_iter", [0])

            def emit_p1(m):
                ccx, hg = m // 4, m % 4
                heads = [hg * 4 + i for i in range(4)]
                p = m % 2
                a_bank = bank(2 if p == 0 else 5)
                ktr0 = 512 + p * 1024

                def t_p1(t, ccx=ccx, heads=heads, a_bank=a_bank, ktr0=ktr0, m=m,
                         qtgt=qkv_done_v, stgt=qkv_done_s, q2tgt=qk2_done,
                         atgt=att_done_prev, vtgt=(vca_v1.get(m - 2, 0)),
                         kutgt=scku_m.get(m - 2, 0)):
                    if m < 2:
                        t.wait_ge(S("vcq").h, qtgt)
                        t.wait_ge(S("scq").h, stgt)
                        t.wait_ge(S("sld").h, q2tgt)
                        if atgt:
                            t.wait_ge(S("vca").h, atgt)
                    else:
                        t.wait_ge(S("vca").h, vtgt)
                        t.wait_ge(S("scku").h, kutgt)
                    for hi, h in enumerate(heads):
                        kt = hsl(kt_sb, kt2_sb, h, ccx)
                        qt = hsl(qt_sb, qt2_sb, h, ccx)
                        t.matmul(a_bank[:, hi * C:(hi + 1) * C], kt, qt,
                                 start=True, stop=True)
                    for hi, h in enumerate(heads):
                        kt = hsl(kt_sb, kt2_sb, h, ccx)
                        ins = t.transpose(psb[:, ktr0 + hi * HE:ktr0 + (hi + 1) * HE],
                                          kt, eye[0:HE, 0:HE])
                    ins.then_inc(S("pea").h, 1)
                S("pea").n += 1
                pea_p1[m] = S("pea").n
                em("tensor", t_p1)

            def emit_p2(m):
                ccx, hg = m // 4, m % 4
                heads = [hg * 4 + i for i in range(4)]
                p = m % 2
                oio_bank = bank(3 if p == 0 else 0)
                u_bank = bank(4 if p == 0 else 1)
                amr = am2[:, p * 4 * C:(p + 1) * 4 * C]

                kur0 = (m % 2) * 4 * HE

                def t_p2(t, ccx=ccx, heads=heads, oio=oio_bank, ub=u_bank, amr=amr,
                         kur0=kur0, vtgt=vca_v1[m], kutgt=scku_m[m],
                         tbtgt=sctb_m.get(ab_iter[0] + m - 4, 0)):
                    t.wait_ge(S("vca").h, vtgt)
                    t.wait_ge(S("scku").h, kutgt)
                    if tbtgt:
                        t.wait_ge(S("sctb").h, tbtgt)
                    for hi, h in enumerate(heads):
                        t.matmul(oio[:, hi * HE:(hi + 1) * HE],
                                 amr[:, hi * C:(hi + 1) * C],
                                 v_sb[:, ccx * D + h * HE:ccx * D + (h + 1) * HE],
                                 start=True, stop=True)
                    for hi, h in enumerate(heads):
                        tr, tc = tsl(h)
                        qt = hsl(qt_sb, qt2_sb, h, ccx)
                        t.matmul(oio[:, 256 + hi * HE:256 + (hi + 1) * HE], qt,
                                 tbf[tr, tc], start=True, stop=True)
                    for hi, h in enumerate(heads):
                        ins = t.matmul(ub[0:HE, hi * HE:(hi + 1) * HE],
                                       ku_sb[:, kur0 + hi * HE:kur0 + (hi + 1) * HE],
                                       v_sb[:, ccx * D + h * HE:ccx * D + (h + 1) * HE],
                                       start=True, stop=True)
                    ins.then_inc(S("pea").h, 1)
                S("pea").n += 1
                pea_p2[m] = S("pea").n
                em("tensor", t_p2)

            def emit_v1(m):
                ccx, hg = m // 4, m % 4
                heads = [hg * 4 + i for i in range(4)]
                p = m % 2
                a_bank = bank(2 if p == 0 else 5)
                ktr0 = 512 + p * 1024
                amr = am2[:, p * 4 * C:(p + 1) * 4 * C]
                kur0 = p * 4 * HE

                def v_1(v, hg=hg, heads=heads, a_bank=a_bank, amr=amr,
                        tgt=pea_p1[m], stgt=(spill_tgt_prev if m == 0 else 0)):
                    v.wait_ge(S("pea").h, tgt)
                    if stgt:
                        v.wait_ge(S("sst").h, stgt)
                    ins = v.tensor_tensor(amr[:, :], a_bank,
                                          dmask[:, hg * 4 * C:(hg * 4 + 4) * C], ALU.mult)
                    ins.then_inc(S("vca").h, 1)
                S("vca").n += 1
                vca_v1[m] = S("vca").n
                em("vector", v_1)

                def s_ku(sc, heads=heads, ktr0=ktr0, kur0=kur0, tgt=pea_p1[m]):
                    sc.wait_ge(S("pea").h, tgt)
                    sc.drain()
                    for hi, h in enumerate(heads):
                        ins = sc.mul(ku_sb[:, kur0 + hi * HE:kur0 + (hi + 1) * HE],
                                     psb[:, ktr0 + hi * HE:ktr0 + (hi + 1) * HE],
                                     gcols[:, NH + h:NH + h + 1])
                    ins.then_inc(S("scku").h, 1)
                S("scku").n += 1
                scku_m[m] = S("scku").n
                em("scalar", s_ku)

            def emit_v2(m):
                ccx, hg = m // 4, m % 4
                heads = [hg * 4 + i for i in range(4)]
                p = m % 2
                oio_bank = bank(3 if p == 0 else 0)
                u_bank = bank(4 if p == 0 else 1)

                def v_2(v, ccx=ccx, heads=heads, oio=oio_bank, ub=u_bank,
                        oir0=(m % 2) * 4 * HE, tgt=pea_p2[m]):
                    v.wait_ge(S("pea").h, tgt)
                    v.tensor_copy(oicp[:, oir0:oir0 + 4 * HE], oio[:, 0:4 * HE])
                    v.drain()
                    for hi, h in enumerate(heads):
                        v.scalar_tensor_tensor(
                            comb_sb[:, ccx * D + h * HE:ccx * D + (h + 1) * HE],
                            oio[:, 256 + hi * HE:256 + (hi + 1) * HE],
                            gcols[:, h:h + 1],
                            oicp[:, oir0 + hi * HE:oir0 + (hi + 1) * HE],
                            ALU.mult, ALU.add)
                    for hi, h in enumerate(heads):
                        tr, tc = tsl(h)
                        ins = v.scalar_tensor_tensor(tf32[tr, tc], tf32[tr, tc], gC_imm[h],
                                                     ub[0:HE, hi * HE:(hi + 1) * HE],
                                                     ALU.mult, ALU.add)
                    ins.then_inc(S("vca").h, 1)
                S("vca").n += 1
                vca_v2[m] = S("vca").n
                em("vector", v_2)

                def s_tb(sc, heads=heads, tgt=vca_v2[m]):
                    sc.wait_ge(S("vca").h, tgt)
                    sc.drain()
                    for hi, h in enumerate(heads):
                        tr, tc = tsl(h)
                        ins = sc.copy(tbf[tr, tc], tf32[tr, tc])
                    ins.then_inc(S("sctb").h, 1)
                S("sctb").n += 1
                sctb_m[ab_iter[0] + m] = S("sctb").n
                em("scalar", s_tb)

            for m in range(16):
                emit_p1(m)
                emit_v1(m)
                if m >= 1:
                    emit_p2(m - 1)
                    emit_v2(m - 1)
            emit_p2(15)
            emit_v2(15)
            ab_iter[0] += 16
            att_done = S("vca").n
            att_done_prev = att_done

            S("sst").n += 64
            spill_tgt_prev = S("sst").n

            def spill(s, c=c, tgt=att_done, stgt=S("sst").n):
                s.wait_ge(S("vca").h, tgt)
                for tt in range(4):
                    ins = s.dma_start(out=comb_h[c * SC + tt * C:c * SC + (tt + 1) * C, :],
                                      in_=comb_sb[:, tt * D:(tt + 1) * D])
                    ins.then_inc(S("sst").h, 16)
                s.wait_ge(S("sst").h, stgt)
            em("sync", spill)

            def vstat(v, tgt=att_done):
                v.wait_ge(S("vca").h, tgt)
                src3 = comb_sb[:, :].rearrange("p (t d) -> p t d", t=4)
                for h in range(NH):
                    ins = v.reduce_sum(stattmp[:, h:h + 1],
                                       src3[:, :, h * HE:(h + 1) * HE], axis=AX.XY)
                ins.then_inc(S("vst").h, 1)
            S("vst").n += 1
            em("vector", vstat)

            def sstat(sc, tgt=att_done):
                sc.wait_ge(S("vca").h, tgt)
                sc.drain()
                src3 = comb_sb[:, :].rearrange("p (t d) -> p t d", t=4)
                for h in range(NH):
                    if h >= 2:
                        sc.drain()
                    ins = sc.activation(sqdump[:, (h % 2) * 256:(h % 2) * 256 + 256],
                                        src3[:, :, h * HE:(h + 1) * HE],
                                        AF.Square, accum_out=stattmp[:, 16 + h:17 + h])
                ins.then_inc(S("scst").h, 1)
            S("scst").n += 1
            em("scalar", sstat)

            def vacc(v, tgt=S("scst").n, tgt2=S("vst").n):
                v.wait_ge(S("scst").h, tgt)
                v.wait_ge(S("vst").h, tgt2)
                v.drain()
                ins = v.tensor_add(statm[:, :], statm[:, :], stattmp[:, :])
                ins.then_inc(S("vst").h, 1)
            S("vst").n += 1
            em("vector", vacc)


        def gn_t(t, tgt=S("vst").n):
            t.wait_ge(S("vst").h, tgt)
            ins = t.matmul(bank(2)[0:1, 0:32], onecol[:, :], statm[:, :],
                           start=True, stop=True)
            ins.then_inc(S("pgn").h, 1)
        S("pgn").n += 1
        em("tensor", gn_t)

        def gn_v1(v, tgt=S("pgn").n):
            v.wait_ge(S("pgn").h, tgt)
            v.tensor_scalar_mul(srow[:, :], bank(2)[0:1, 0:32], 1.0 / GNN)
            v.drain()
            v.tensor_mul(srow2[:, :], srow[0:1, 0:16], srow[0:1, 0:16])
            v.drain()
            ins = v.tensor_sub(srow2[:, :], srow[0:1, 16:32], srow2[:, :])
            ins.then_inc(S("vgn").h, 1)
        S("vgn").n += 1
        em("vector", gn_v1)

        def gn_s(sc, tgt=S("vgn").n):
            sc.wait_ge(S("vgn").h, tgt)
            sc.drain()
            sc.activation(srow2[:, :], srow2[:, :], AF.Ln, bias=cst[0:1, 0:1])
            sc.drain()
            ins = sc.activation(srow[0:1, 16:32], srow2[:, :], AF.Exp, scale=cst[0:1, 1:2])
            ins.then_inc(S("scg2").h, 1)
        S("scg2").n += 1
        em("scalar", gn_s)

        def gn_t2(t, tgt=S("scg2").n):
            t.wait_ge(S("scg2").h, tgt)
            ins = t.matmul(bank(2)[:, 0:32], onerow[:, :], srow[:, :],
                           start=True, stop=True)
            ins.then_inc(S("pgn").h, 1)
        S("pgn").n += 1
        em("tensor", gn_t2)

        def gn_v3(v, tgt=S("pgn").n):
            v.wait_ge(S("pgn").h, tgt)
            v.tensor_copy(bcast[:, :], bank(2)[:, 0:32])
            v.drain()
            for half in range(2):
                rows = slice(half * HE, (half + 1) * HE)
                v.tensor_mul(s12[rows, 0:8], bcast[rows, 16 + half:32:2], gnwb[rows, 0:8])
            v.drain()
            for half in range(2):
                rows = slice(half * HE, (half + 1) * HE)
                v.tensor_mul(s12[rows, 8:16], bcast[rows, half:16:2], s12[rows, 0:8])
            v.drain()
            ins = v.tensor_sub(s12[:, 8:16], gnwb[:, 8:16], s12[:, 8:16])
            ins.then_inc(S("vgn").h, 1)
        S("vgn").n += 1
        em("vector", gn_v3)
        gn_done = S("vgn").n
        if p1_att and p1_qkv and p1_gn and p1_stats:
            em.flush_block()
        else:
            em.q = {k: [] for k in ENGINES}

    if (2 not in phases) or not (p1_att and p1_qkv and p1_gn and p1_stats):
        es.close()
        return nc

    with contextlib.ExitStack() as p2es:
        sb2 = lambda name, shape, dt: p2es.enter_context(nc.sbuf_tensor(name, shape, dt))
        ps2 = lambda name, shape, dt: p2es.enter_context(nc.psum_tensor(name, shape, dt))

        combt_sb = sb2("combt_sb", [C, 4 * D], BF16)
        combT = sb2("combT", [C, 8 * SC], BF16)
        x1_sb = sb2("x1_sb", [C, 2 * 4 * D], F32)
        xln2_bf = sb2("xln2_bf", [C, 4 * D], BF16)
        xln2T = sb2("xln2T", [C, 8 * SC], BF16)
        wring = sb2("wring", [C, 3 * WSLOT], BF16)
        h_sb = sb2("h_sb", [C, 2 * NIT * SC], BF16)
        silu_sb = sb2("silu_sb", [C, 2 * SC], BF16)
        ucp_sb = sb2("ucp_sb", [C, 2 * SC], BF16)
        el_sb = sb2("el_sb", [C, 8], F32)
        cmask = sb2("cmask", [C, 8], F32)
        c8_sb = sb2("c8_sb", [C, 4 * NEXP], F32)
        rcol = sb2("rcol", [C, 8], F32)

        psf2 = ps2("psf2", [C, 6 * 512], F32)
        psb2 = ps2("psb2", [C, 2048], BF16)
        bank2 = lambda i: psf2[:, i * 512:(i + 1) * 512]

        def conv_src2(p):
            e, j = p // 3, p % 3
            if e < 8:
                hs = (wg_h, wu_h, wd_h)[j]
                off = e * D * IFF
            else:
                hs = (swg_h, swu_h, swd_h)[j]
                off = 0
            if j < 2:
                return bass.AP(hs, off, [[IFF, C], [C * IFF, 8], [1, IFF]])
            return bass.AP(hs, off, [[D, C], [C * D, NIT], [1, D]])

        piece_done = {}
        piece_tgt = {}

        def emit_piece_load(p):
            wait = piece_done.get(p - 3)
            sname = "plds%d" % (p % 3)
            S(sname).n += 16
            piece_tgt[p] = (sname, S(sname).n)

            def gload(g, p=p, wait=wait, sname=sname, first=(p == 0),
                      cv=(S("gcs0").n, S("gcs1").n)):
                if first and use_conv:
                    g.wait_ge(S("gcs0").h, cv[0])
                    g.wait_ge(S("gcs1").h, cv[1])
                if wait is not None:
                    g.wait_ge(S(wait[0]).h, wait[1])
                ins = g.dma_start(out=wring[:, (p % 3) * WSLOT:(p % 3) * WSLOT + WSLOT],
                                  in_=(wbf_h[p % PPT] if use_conv else conv_src2(p % PPT)))
                ins.then_inc(S(sname).h, 16)
            em("gpsimd", gload)

        for p in range(3):
            emit_piece_load(p)
        next_piece = 3

        def emit_t512b(src_t, dst, scale_cols=None, shift_cols=None):
            pe, vc = S("pet"), S("vct")
            for dt in range(8):
                reg = psb2[:, (dt % 2) * 1024:(dt % 2) * 1024 + 512]

                def ttr(t, dt=dt, reg=reg, banktgt=vc.n - 1):
                    if banktgt >= 1:
                        t.wait_ge(vc.h, banktgt)
                    for tt in range(4):
                        ins = t.transpose(reg[:, tt * C:(tt + 1) * C],
                                          src_t[:, tt * D + dt * C:tt * D + (dt + 1) * C],
                                          eye[:, :])
                    ins.then_inc(pe.h, 1)
                pe.n += 1
                em("tensor", ttr)

                def tev(v, dt=dt, reg=reg, tgt=pe.n):
                    v.wait_ge(pe.h, tgt)
                    if scale_cols is not None:
                        ins = v.tensor_scalar(dst[:, dt * SC:(dt + 1) * SC], reg[:, :],
                                              scale_cols[:, dt:dt + 1], shift_cols[:, dt:dt + 1],
                                              ALU.mult, ALU.add)
                    else:
                        ins = v.tensor_copy(dst[:, dt * SC:(dt + 1) * SC], reg[:, :])
                    ins.then_inc(vc.h, 1)
                vc.n += 1
                em("vector", tev)

        store_tgts = []

        for t_i in range(NTILE):
            x1 = x1_sb[:, (t_i % 2) * 4 * D:(t_i % 2 + 1) * 4 * D]

            if t_i == 0:
                S("sld").n += 128

                def ld0(s, tgt=S("sld").n):
                    for tt in range(4):
                        s.dma_start(out=combt_sb[:, tt * D:(tt + 1) * D],
                                    in_=comb_h[tt * C:(tt + 1) * C, :]).then_inc(S("sld").h, 16)
                        ins = s.dma_start(out=x_sb[:, tt * D:(tt + 1) * D],
                                          in_=x_h[tt * C:(tt + 1) * C, :])
                        ins.then_inc(S("sld").h, 16)
                    s.wait_ge(S("sld").h, tgt)
                em("sync", ld0)
            combx_tgt = S("sld").n

            def twx2(t, tgt=combx_tgt, g=gn_done, first=(t_i == 0)):
                t.wait_ge(S("sld").h, tgt)
                if first:
                    t.wait_ge(S("vgn").h, g)
            em("tensor", twx2)

            em("vector", lambda v, tgt=combx_tgt: v.wait_ge(S("sld").h, tgt))
            em("scalar", lambda sc, tgt=combx_tgt: (sc.wait_ge(S("sld").h, tgt), sc.drain()))

            emit_t512b(combt_sb, combT, s12[:, 0:8], s12[:, 8:16])
            combT_done = S("vct").n
            pet_tr_done = S("pet").n

            if t_i + 1 < NTILE:
                S("sld").n += 64

                def ldc(s, t_n=t_i + 1, ptgt=pet_tr_done, tgt=S("sld").n):
                    s.wait_ge(S("pet").h, ptgt)
                    for tt in range(4):
                        ins = s.dma_start(out=combt_sb[:, tt * D:(tt + 1) * D],
                                          in_=comb_h[t_n * SC + tt * C:t_n * SC + (tt + 1) * C, :])
                        ins.then_inc(S("sld").h, 16)
                    s.wait_ge(S("sld").h, tgt)
                em("sync", ldc)

            vcw_tloc = []
            for tloc in range(4):
                for dc in range(2):
                    gidx = tloc * 2 + dc
                    bk = bank2(gidx % 2)

                    def twout(t, tloc=tloc, dc=dc, bk=bk, gidx=gidx,
                              ctgt=combT_done, banktgt=S("vcw").n - 1,
                              dtgt=S("vcd").n):
                        if gidx == 0:
                            t.wait_ge(S("vct").h, ctgt)
                            if dtgt >= 1:
                                t.wait_ge(S("vcd").h, dtgt)
                        if banktgt >= 1:
                            t.wait_ge(S("vcw").h, banktgt)
                        for dt in range(8):
                            ins = t.matmul(bk,
                                           combT[:, dt * SC + tloc * C:dt * SC + (tloc + 1) * C],
                                           wout_bf[:, dt * D + dc * 512:dt * D + (dc + 1) * 512],
                                           start=(dt == 0), stop=(dt == 7))
                        ins.then_inc(S("pew").h, 1)
                    S("pew").n += 1
                    em("tensor", twout)

                    def vwout(v, tloc=tloc, dc=dc, bk=bk, tgt=S("pew").n, x1=x1,
                              sttgt=(store_tgts[t_i - 2] if (gidx == 0 and t_i >= 2) else None)):
                        v.wait_ge(S("pew").h, tgt)
                        if sttgt is not None:
                            v.wait_ge(S(sttgt[0]).h, sttgt[1])
                        ins = v.tensor_add(x1[:, tloc * D + dc * 512:tloc * D + (dc + 1) * 512],
                                           x_sb[:, tloc * D + dc * 512:tloc * D + (dc + 1) * 512],
                                           bk)
                        ins.then_inc(S("vcw").h, 1)
                    S("vcw").n += 1
                    em("vector", vwout)
                vcw_tloc.append(("vcw", S("vcw").n))
            wout_done = S("vcw").n

            if t_i + 1 < NTILE:
                S("sld").n += 64

                def ldx2(s, t_n=t_i + 1, wtgt=wout_done, tgt=S("sld").n):
                    s.wait_ge(S("vcw").h, wtgt)
                    for tt in range(4):
                        ins = s.dma_start(out=x_sb[:, tt * D:(tt + 1) * D],
                                          in_=x_h[t_n * SC + tt * C:t_n * SC + (tt + 1) * C, :])
                        ins.then_inc(S("sld").h, 16)
                    s.wait_ge(S("sld").h, tgt)
                em("sync", ldx2)

            ln2_done = emit_ln_sc(x1, xln2_bf, per_tt_wait=vcw_tloc)

            em("tensor", lambda t, tgt=ln2_done: t.wait_ge(S("sc1").h, tgt))
            emit_t512b(xln2_bf, xln2T)
            xln2T_done = S("vct").n

            rreg = [(0, 0), (1, 0), (0, 16), (1, 16)]
            rd_tgts = {}

            def emit_trt(tloc, wait_scr=None):
                rb, rc0 = rreg[tloc]

                def trt(t, tloc=tloc, rb=rb, rc0=rc0, ctgt=xln2T_done,
                        wtgt=wout_done, wait_scr=wait_scr):
                    if tloc == 0:
                        t.wait_ge(S("vct").h, ctgt)
                        t.wait_ge(S("vcw").h, wtgt)
                    if wait_scr is not None:
                        t.wait_ge(S("scr").h, wait_scr)
                    for dt in range(8):
                        ins = t.matmul(bank2(rb)[:, rc0:rc0 + 9],
                                       xln2T[:, dt * SC + tloc * C:dt * SC + (tloc + 1) * C],
                                       gate9_bf[:, dt * 9:(dt + 1) * 9],
                                       start=(dt == 0), stop=(dt == 7))
                    ins.then_inc(S("per").h, 1)
                S("per").n += 1
                em("tensor", trt)

            emit_trt(0)
            emit_trt(1)
            for tloc in range(4):
                rb, rc0 = rreg[tloc]

                def vr1(v, rb=rb, rc0=rc0, tgt=(S("per").n - 1 + (tloc if tloc < 2 else 1))):
                    v.wait_ge(S("per").h, tgt)
                    ins = v.reduce_max(rcol[:, 0:1], bank2(rb)[:, rc0:rc0 + 8], axis=AX.X,
                                       negate=True)
                    ins.then_inc(S("vcr").h, 1)
                S("vcr").n += 1
                em("vector", vr1)

                def sr1(sc, rb=rb, rc0=rc0, tgt=S("vcr").n):
                    sc.wait_ge(S("vcr").h, tgt)
                    sc.activation(el_sb[:, 0:8], bank2(rb)[:, rc0:rc0 + 8], AF.Exp,
                                  bias=rcol[:, 0:1])
                    ins = sc.activation(rcol[:, 1:2], bank2(rb)[:, rc0 + 8:rc0 + 9], AF.Exp,
                                        scale=cst[:, 2:3])
                    ins.then_inc(S("scr").h, 1)
                S("scr").n += 1
                em("scalar", sr1)
                rd_tgts[tloc] = S("scr").n

                def vr2(v, tloc=tloc, tgt=S("scr").n):
                    v.wait_ge(S("scr").h, tgt)
                    v.drain()
                    v.tensor_scalar_add(rcol[:, 2:3], rcol[:, 1:2], 1.0)
                    v.drain()
                    v.reciprocal(c8_sb[:, tloc * NEXP + 8:tloc * NEXP + 9], rcol[:, 2:3])
                    v.reduce_max(rcol[:, 3:4], el_sb[:, 0:8], axis=AX.X)
                    v.drain()
                    v.tensor_scalar(cmask[:, :], el_sb[:, 0:8], rcol[:, 3:4], None, ALU.is_ge)
                    v.drain()
                    v.scalar_tensor_tensor(cmask[:, :], cmask[:, :], -1000.0, el_sb[:, 0:8],
                                           ALU.mult, ALU.add)
                    v.drain()
                    v.reduce_max(rcol[:, 4:5], cmask[:, :], axis=AX.X)
                    v.drain()
                    v.tensor_add(rcol[:, 5:6], rcol[:, 3:4], rcol[:, 4:5])
                    v.drain()
                    v.reciprocal(rcol[:, 6:7], rcol[:, 5:6])
                    v.tensor_scalar(cmask[:, :], el_sb[:, 0:8], rcol[:, 4:5], None, ALU.is_ge)
                    v.drain()
                    ins = v.scalar_tensor_tensor(c8_sb[:, tloc * NEXP:tloc * NEXP + 8],
                                                 el_sb[:, 0:8], rcol[:, 6:7], cmask[:, :],
                                                 ALU.mult, ALU.mult)
                    ins.then_inc(S("vcr").h, 1)
                S("vcr").n += 1
                em("vector", vr2)
                if tloc < 2:
                    emit_trt(tloc + 2, wait_scr=rd_tgts[tloc])
            router_done = S("vcr").n
            router_sc_done = S("scr").n

            for e in range(NEXP):
                p0 = PPT * t_i + 3 * e
                hb = (e % 2) * NIT * SC
                for it in range(NIT):
                    gb = bank2(2 + (it % 2) * 2)
                    ub = bank2(3 + (it % 2) * 2)
                    wg0 = (p0 % 3) * WSLOT
                    wu0 = ((p0 + 1) % 3) * WSLOT

                    def tgu(t, it=it, gb=gb, ub=ub, wg0=wg0, wu0=wu0,
                            ldg=piece_tgt[p0], ldu=piece_tgt[p0 + 1],
                            banktgt=S("scg").n - 1):
                        if it == 0:
                            t.wait_ge(S(ldg[0]).h, ldg[1])
                            t.wait_ge(S(ldu[0]).h, ldu[1])
                        if banktgt >= 1:
                            t.wait_ge(S("scg").h, banktgt)
                        for dt in range(8):
                            t.matmul(gb, wring[:, wg0 + dt * IFF + it * C:wg0 + dt * IFF + (it + 1) * C],
                                     xln2T[:, dt * SC:(dt + 1) * SC],
                                     start=(dt == 0), stop=(dt == 7))
                        for dt in range(8):
                            ins = t.matmul(ub, wring[:, wu0 + dt * IFF + it * C:wu0 + dt * IFF + (it + 1) * C],
                                           xln2T[:, dt * SC:(dt + 1) * SC],
                                           start=(dt == 0), stop=(dt == 7))
                        ins.then_inc(S("pe2").h, 1)
                    S("pe2").n += 1
                    em("tensor", tgu)

                    def sgu(sc, it=it, gb=gb, ub=ub, tgt=S("pe2").n,
                            vtgt=S("vch").n - 1):
                        sc.wait_ge(S("pe2").h, tgt)
                        if vtgt >= 1:
                            sc.wait_ge(S("vch").h, vtgt)
                        sc.activation(silu_sb[:, (it % 2) * SC:(it % 2 + 1) * SC], gb, AF.Silu)
                        ins = sc.copy(ucp_sb[:, (it % 2) * SC:(it % 2 + 1) * SC], ub)
                        ins.then_inc(S("scg").h, 1)
                    S("scg").n += 1
                    em("scalar", sgu)

                    def vh(v, it=it, tgt=S("scg").n, hb=hb, dtgt=max(0, S("ped").n - 8)):
                        v.wait_ge(S("scg").h, tgt)
                        if it == 0 and dtgt >= 1:
                            v.wait_ge(S("ped").h, dtgt)
                        ins = v.tensor_mul(h_sb[:, hb + it * SC:hb + (it + 1) * SC],
                                           silu_sb[:, (it % 2) * SC:(it % 2 + 1) * SC],
                                           ucp_sb[:, (it % 2) * SC:(it % 2 + 1) * SC])
                        ins.then_inc(S("vch").h, 1)
                    S("vch").n += 1
                    em("vector", vh)
                h_done = S("vch").n
                piece_done[p0] = ("scg", S("scg").n)
                piece_done[p0 + 1] = ("scg", S("scg").n)
                while next_piece < NPIECES and (next_piece - 3) in piece_done:
                    emit_piece_load(next_piece)
                    next_piece += 1

                for tloc in range(4):
                    for dc in range(2):
                        didx = tloc * 2 + dc
                        db = bank2(didx % 2)
                        wd0 = ((p0 + 2) % 3) * WSLOT

                        def tdown(t, tloc=tloc, dc=dc, db=db, wd0=wd0, didx=didx, e=e,
                                  htgt=h_done, ldd=piece_tgt[p0 + 2], hb=hb,
                                  banktgt=(S("vcd").n - 1 if (e > 0 or didx >= 2) else -1),
                                  sctgt=router_sc_done):
                            if didx == 0:
                                t.wait_ge(S("vch").h, htgt)
                                t.wait_ge(S(ldd[0]).h, ldd[1])
                                if e == 0:
                                    t.wait_ge(S("scr").h, sctgt)
                            if banktgt >= 1:
                                t.wait_ge(S("vcd").h, banktgt)
                            for it in range(NIT):
                                ins = t.matmul(db,
                                               h_sb[:, hb + it * SC + tloc * C:hb + it * SC + (tloc + 1) * C],
                                               wring[:, wd0 + it * D + dc * 512:wd0 + it * D + (dc + 1) * 512],
                                               start=(it == 0), stop=(it == NIT - 1))
                            ins.then_inc(S("ped").h, 1)
                        S("ped").n += 1
                        em("tensor", tdown)

                        def vdown(v, tloc=tloc, dc=dc, db=db, e=e, didx=didx,
                                  tgt=S("ped").n, lntgt=ln2_done, x1=x1,
                                  rtgt=router_done):
                            v.wait_ge(S("ped").h, tgt)
                            if e == 0 and didx == 0:
                                v.wait_ge(S("sc1").h, lntgt)
                                v.wait_ge(S("vcr").h, rtgt)
                            ccol = c8_sb[:, tloc * NEXP + min(e, 8):tloc * NEXP + min(e, 8) + 1]
                            ins = v.scalar_tensor_tensor(
                                x1[:, tloc * D + dc * 512:tloc * D + (dc + 1) * 512],
                                db, ccol,
                                x1[:, tloc * D + dc * 512:tloc * D + (dc + 1) * 512],
                                ALU.mult, ALU.add)
                            ins.then_inc(S("vcd").h, 1)
                        S("vcd").n += 1
                        em("vector", vdown)
                piece_done[p0 + 2] = ("ped", S("ped").n)
                while next_piece < NPIECES and (next_piece - 3) in piece_done:
                    emit_piece_load(next_piece)
                    next_piece += 1

            tile_done = S("vcd").n
            ssem = "sstA" if t_i % 2 == 0 else "sstB"
            S(ssem).n += 64
            store_tgts.append((ssem, S(ssem).n))

            def st(s, t_i=t_i, tgt=tile_done, ssem=ssem,
                   fin=(S("sstA").n, S("sstB").n), x1=x1):
                s.wait_ge(S("vcd").h, tgt)
                for tt in range(4):
                    ins = s.dma_start(out=out_h[t_i * SC + tt * C:t_i * SC + (tt + 1) * C, :],
                                      in_=x1[:, tt * D:(tt + 1) * D])
                    ins.then_inc(S(ssem).h, 16)
                if t_i == NTILE - 1:
                    s.wait_ge(S("sstA").h, fin[0])
                    s.wait_ge(S("sstB").h, fin[1])
            em("sync", st)

        em.flush_block()

    es.close()
    return nc
def _make_consts():
    g64 = _gammas64()
    iota = np.arange(C, dtype=np.float64)
    diff = iota[:, None] - iota[None, :]
    dmask = np.zeros((C, NH * C), dtype=np.float32)
    gcols = np.zeros((C, 2 * NH), dtype=np.float32)
    for h in range(NH):
        g = g64[h]
        Dm = np.where(diff >= 0, g ** np.maximum(diff, 0) / 8.0, 0.0)
        dmask[:, h * C:(h + 1) * C] = Dm.T.astype(np.float32)
        gcols[:, h] = (g ** iota / 8.0).astype(np.float32)
        gcols[:, NH + h] = (g ** (C - iota)).astype(np.float32)
    eye = np.eye(C, dtype=ml_dtypes.bfloat16)
    return dmask, gcols, eye


def kernel(**inputs):
    f32 = lambda k: np.ascontiguousarray(np.asarray(inputs[k], dtype=np.float32))
    x = f32("L0_attn_x")
    wq = np.ascontiguousarray(np.transpose(f32("L0_attn_Wq"), (1, 0, 2)).reshape(D, D))
    wk = np.ascontiguousarray(np.transpose(f32("L0_attn_Wk"), (1, 0, 2)).reshape(D, D))
    wv = np.ascontiguousarray(np.transpose(f32("L0_attn_Wv"), (1, 0, 2)).reshape(D, D))
    wout = f32("L0_attn_Wout")
    gate9 = np.ascontiguousarray(np.concatenate(
        [f32("L0_ffn_gate_w").T, f32("L0_ffn_shared_gate_w").T], axis=1))
    gnw = f32("L0_attn_gn_weight").reshape(8, C).T
    gnb = f32("L0_attn_gn_bias").reshape(8, C).T
    gnwb = np.ascontiguousarray(np.concatenate([gnw, gnb], axis=1))
    dmask, gcols, eye = _make_consts()

    nc = build_nc()
    shared = dict(wq=wq, wk=wk, wv=wv, wout=wout, gate9=gate9, gnwb=gnwb,
                  wg=f32("L0_ffn_w_gate"), wu=f32("L0_ffn_w_up"), wd=f32("L0_ffn_w_down"),
                  swg=f32("L0_ffn_sw_gate"), swu=f32("L0_ffn_sw_up"),
                  swd=f32("L0_ffn_sw_down"),
                  dmask=dmask, gcols=gcols, eye=eye)
    in_maps = [dict(shared, x=np.ascontiguousarray(x[b])) for b in range(8)]
    res = run_bass_kernel_spmd(nc, in_maps, core_ids=list(range(8)))
    return np.stack([np.asarray(r["out"], dtype=np.float32) for r in res.results], axis=0)
